# version 25
# baseline (speedup 1.0000x reference)
import math
import numpy as np

import concourse.bass as bass
import concourse.mybir as mybir
from concourse.bass_utils import run_bass_kernel_spmd

F32 = mybir.dt.float32
BF16 = mybir.dt.bfloat16
I32 = mybir.dt.int32
AF = mybir.ActivationFunctionType
ALU = mybir.AluOpType
AX = mybir.AxisListType

NCORES = 8
B_FULL, T, D = 16, 2048, 1024
B_CORE = B_FULL // NCORES
TOK = B_CORE * T
H = 8
GT = 512
NG = TOK // GT
NG_SEQ = T // GT
CH = 128
NCH = GT // CH
DI = 3072
NOB = DI // 128
NJ = D // 128
LNC = -0.5 * math.log(128.0)
LN65536 = math.log(65536.0)
EPS = 1e-5

STAGE = "full"
DBG = None


class Sem:
    def __init__(self, nc, name):
        self.h = nc.alloc_semaphore(name)
        self.n = 0


class Q:

    def __init__(self, eng, sem, serialize=True, dma=False):
        self.e = eng
        self.s = sem
        self.serialize = serialize
        self.step = 16 if dma else 1

    def __getattr__(self, name):
        f = getattr(self.e, name)

        def g(*a, **k):
            if self.serialize and self.s.n > 0:
                self.e.wait_ge(self.s.h, self.s.n)
            ins = f(*a, **k)
            ins.then_inc(self.s.h, self.step)
            self.s.n += self.step
            return ins

        return g

    def wait(self, sem, n=None):
        self.e.wait_ge(sem.h, sem.n if n is None else n)


def _load_act_set(nc, scalar):
    from concourse.hw_specs import get_activation_tables
    for idx, (name, funcs) in enumerate(get_activation_tables(nc.m.arch).items()):
        if AF.Exp in funcs and AF.Ln in funcs:
            return scalar.add_instruction(
                mybir.InstLoadActFuncSet(
                    name=nc.get_next_instruction_name(),
                    act_func_set_id=idx, ins=[], outs=[]))
    raise ValueError("no table set with Exp+Ln")


def build(stage=None, ncores=NCORES):
    stage = stage or STAGE
    nc = bass.Bass("TRN2", target_bir_lowering=False, debug=False)

    xT = nc.dram_tensor("xT", [D, TOK], F32, kind="ExternalInput").ap()
    wqT = nc.dram_tensor("wqT", [D, D], F32, kind="ExternalInput").ap()
    wfT = nc.dram_tensor("wfT", [D, D], F32, kind="ExternalInput").ap()
    wiT = nc.dram_tensor("wiT", [D, D], F32, kind="ExternalInput").ap()
    woT = nc.dram_tensor("woT", [D, D], F32, kind="ExternalInput").ap()
    UPC = 2 * NOB // ncores
    wdq = nc.dram_tensor("wdq", [UPC * 128, D], F32, kind="ExternalInput").ap()
    WUPC = NJ * 3 // ncores
    wdq2 = nc.dram_tensor("wdq2", [WUPC * 128, D], F32,
                          kind="ExternalInput").ap()
    c_gw = nc.dram_tensor("c_gw", [128, NJ], F32, kind="ExternalInput").ap()
    c_idn = nc.dram_tensor("c_idn", [128, 128], BF16, kind="ExternalInput").ap()
    c_idnf = nc.dram_tensor("c_idnf", [128, 128], F32, kind="ExternalInput").ap()
    c_mask = nc.dram_tensor("c_mask", [128, 128], BF16, kind="ExternalInput").ap()
    c_ones1 = nc.dram_tensor("c_ones1", [1, 128], F32, kind="ExternalInput").ap()
    c_onesk = nc.dram_tensor("c_onesk", [128, GT], BF16, kind="ExternalInput").ap()
    c_oneskb = nc.dram_tensor("c_oneskb", [128, 1], BF16, kind="ExternalInput").ap()

    outT = nc.dram_tensor("outT", [D, TOK], F32, kind="ExternalOutput").ap()
    xp_sp = nc.dram_tensor("xp_sp", [D, TOK], F32, kind="Internal").ap()

    full = stage == "full"
    if full:
        g_sp = nc.dram_tensor("g_sp", [DI, TOK], F32, kind="Internal").ap()
        u_sp = nc.dram_tensor("u_sp", [DI, TOK], F32, kind="Internal").ap()
        t4_sp = nc.dram_tensor("t4_sp", [DI, TOK], F32, kind="Internal").ap()
        t5_sp = nc.dram_tensor("t5_sp", [DI, TOK], F32, kind="Internal").ap()
        _sh = "Shared" if ncores > 4 else "Local"
        wloc = nc.dram_tensor("wloc", [UPC * 1024, 128], BF16,
                              kind="Internal").ap()
        wgat = nc.dram_tensor("wgat", [2 * NOB * 1024, 128], BF16,
                              kind="Internal", addr_space=_sh).ap()
        wdloc = nc.dram_tensor("wdloc", [WUPC * 1024, 128], BF16,
                               kind="Internal").ap()
        wdgat = nc.dram_tensor("wdgat", [NJ * 3 * 1024, 128], BF16,
                               kind="Internal", addr_space=_sh).ap()
        cc_in = [nc.dram_tensor(f"cc_in{i}", [1, 1], F32, kind="Internal").ap()
                 for i in range(5)]
        cc_out = [nc.dram_tensor(f"cc_out{i}", [1, 1], F32, kind="Internal",
                                 addr_space="Shared" if ncores > 4 else "Local"
                                 ).ap() for i in range(5)]

    PSIZE = nc.SBUF_PARTITION_SIZE_BYTES
    cur = [((PSIZE - nc.sbuf_bytes_remaining) + 31) // 32 * 32]

    sb_at = {}

    def sba(name, shape, dtype, at=None):
        esz = 2 if dtype in (BF16, mybir.dt.float16) else 4
        nbytes = int(np.prod(shape[1:])) * esz
        if at is None:
            at = cur[0]
            cur[0] = at + ((nbytes + 31) // 32) * 32
        assert at + nbytes <= PSIZE - 32, f"sbuf overflow {name}: {at + nbytes}"
        sb_at[name] = at
        return nc.alloc_sbuf_tensor_at(name, shape, dtype, offset=at)

    R0 = cur[0]
    wq_s = sba("wq_s", [128, NJ * D], BF16)
    wf_s = sba("wf_s", [128, NJ * D], BF16)
    wi_s = sba("wi_s", [128, NJ * D], BF16)
    wo_s = sba("wo_s", [128, NJ * D], BF16)
    R0_END = cur[0]

    R1 = cur[0]
    xg_s = [sba(f"xg{i}", [128, NJ * GT], F32) for i in range(2)]
    xn_s = sba("xn_s", [128, NJ * GT], BF16)
    of_s = sba("of_s", [128, NJ * GT], F32)
    ofb_s = sba("ofb_s", [128, NJ * GT], BF16)
    st_f = sba("st_f", [128, H * 128], F32)
    st_b = sba("st_b", [128, H * 128], BF16)
    lq_s = sba("lq_s", [128, GT], F32)
    lf_s = sba("lf_s", [128, GT], F32)
    lt_s = sba("lt_s", [128, GT], F32)
    uqt_s = sba("uqt_s", [128, GT], F32)
    tk_s = sba("tk_s", [128, GT], F32)
    vb_s = sba("vb_s", [128, GT], BF16)
    eqa_s = sba("eqa_s", [128, 128], F32)
    eqb_s = sba("eqb_s", [128, 64], F32)
    qta_s = sba("qta_s", [128, 128], BF16)
    qtb_s = sba("qtb_s", [128, 64], BF16)
    kta_s = sba("kta_s", [128, 64], BF16)
    ktb_s = sba("ktb_s", [128, 128], BF16)
    khat_s = sba("khat_s", [128, 128], BF16)
    vT_s = sba("vT_s", [128, 128], BF16)
    khT_s = sba("khT_s", [128, 128], BF16)
    pt_s = sba("pt_s", [128, 128], BF16)
    bias_s = sba("bias_s", [128, 32], F32)
    qp_s = sba("qp_s", [128, GT], F32)
    R1_END = cur[0]

    gw_s = sba("gw_s", [128, NJ], F32)
    mask_s = sba("mask_s", [128, 128], BF16)
    oneskb_s = sba("oneskb_s", [128, 1], BF16)
    onesk_s = sba("onesk_s", [128, GT], BF16)
    pairx = [sba(f"pairx{i}", [128, GT], BF16, at=sb_at["ofb_s"] + i * GT * 2)
             for i in range(2)]
    pairq = [sba(f"pairq{i}", [128, GT], BF16,
                 at=sb_at["ofb_s"] + (2 + i) * GT * 2) for i in range(2)]
    r1bc_s = sba("r1bc_s", [128, GT], F32)
    mr1bc_s = sba("mr1bc_s", [128, GT], F32)
    tmp1_s = sba("tmp1_s", [128, GT], F32, at=sb_at["of_s"])
    srow_s = sba("srow_s", [1, GT], F32)
    vrow_s = sba("vrow_s", [1, GT], F32)
    mrow_s = sba("mrow_s", [1, GT], F32)
    R1X_END = cur[0]
    idn_s = sba("idn_s", [128, 128], BF16)
    ones1_s = sba("ones1_s", [1, 128], F32)
    r2row_s = sba("r2row_s", [1, TOK], F32)
    mr2row_s = sba("mr2row_s", [1, TOK], F32)
    x2max_s = sba("x2max_s", [128, 1], F32)
    gmU_s = sba("gmU_s", [128, 1], F32)
    gmG_s = sba("gmG_s", [128, 1], F32)
    gmT4_s = sba("gmT4_s", [128, 1], F32)
    gmT5_s = sba("gmT5_s", [128, 1], F32)
    redt_s = sba("redt_s", [128, 1], F32)
    sc_s = sba("sc_s", [1, 32], F32)
    vec_s = sba("vec_s", [128, 16], F32)
    idnf_s = sba("idnf_s", [128, 128], F32)
    wraw_s = [sba("wraw0", [128, D], F32)]
    dqt_s = sba("dqt_s", [128, D], F32)
    dqi_s = sba("dqi_s", [128, D], I32, at=sb_at["dqt_s"])
    dqb_s = sba("dqb_s", [128, D], BF16)
    dqm_s = sba("dqm_s", [128, 32], F32)
    dqr_s = sba("dqr_s", [128, 32], F32)
    dqo_s = sba("dqo_s", [128, NJ * 128], BF16)
    R2_END = cur[0]

    if full:
        x1b_s = sba("x1b_s", [128, NJ * TOK], BF16, at=R0)
        wdT_s = sba("wdT_s", [128, NOB * 1024], BF16, at=R0)
        t45g_s = [sba(f"t45g{i}", [128, 4 * GT], F32,
                      at=R0 + NOB * 1024 * 2 + i * 8192) for i in range(2)]
        ost_s = [sba(f"ost{i}", [128, 4 * GT], F32,
                     at=R0 + NOB * 1024 * 2 + i * 8192) for i in range(2)]
        t45a_s = [sba(f"t45a{i}", [128, 4 * GT], F32, at=R1 + i * 8192)
                  for i in range(2)]
        xpr_s = sba("xpr_s", [128, NJ * GT], F32, at=R1)
        rd_s = [sba(f"rd{i}", [128, 4 * GT], F32, at=R1 + 16384 + i * 8192)
                for i in range(4)]
        wt_s = [sba(f"wt{i}", [128, NJ * 128], BF16, at=R1 + 49152 + i * 2048)
                for i in range(2)]
        stg_s = [sba(f"stg{i}", [128, 4 * GT], F32, at=R1 + 53248 + i * 8192)
                 for i in range(2)]
        iiA = (sba("iiA0", [128, GT], F32, at=sb_at["wraw0"]),
               sba("iiA1", [128, GT], F32, at=sb_at["wraw0"] + 2048),
               sba("iiA2", [128, GT], F32, at=sb_at["dqt_s"]))
        iiB = (sba("iiB0", [128, GT], F32, at=sb_at["dqt_s"] + 2048),
               sba("iiB1", [128, GT], F32, at=sb_at["dqb_s"]),
               sba("iiB2", [128, GT], F32, at=sb_at["dqo_s"]))
        iiAi = (sba("iiAi0", [128, GT], I32, at=sb_at["iiA0"]),
                sba("iiAi1", [128, GT], I32, at=sb_at["iiA1"]),
                sba("iiAi2", [128, GT], I32, at=sb_at["iiA2"]))
        iiBi = (sba("iiBi0", [128, GT], I32, at=sb_at["iiB0"]),
                sba("iiBi1", [128, GT], I32, at=sb_at["iiB1"]),
                sba("iiBi2", [128, GT], I32, at=sb_at["iiB2"]))
        o16h_s = sba("o16h", [128, 12 * GT], BF16, at=R1 + 16384 + 16384)
        xf1 = sba("xf1", [128, GT], F32, at=sb_at["qp_s"])
        xfi = sba("xfi", [128, GT], I32, at=sb_at["qp_s"])
        xf2 = (sba("xf2a", [128, GT], F32, at=sb_at["dqb_s"]),
               sba("xf2b", [128, GT], F32, at=sb_at["dqo_s"]))
        gii = (sba("gii0", [128, 1024], I32, at=sb_at["wraw0"]),
               sba("gii1", [128, 1024], I32, at=sb_at["dqt_s"]))
        assert R1 + 77824 + 8192 <= R1_END, (R1, R1_END)
        sfb = [sba(f"sfb{i}", [128, 1024], F32, at=R1 + 69632 + i * 4096)
               for i in range(2)]
        sib = [sba(f"sib{i}", [128, 1024], I32, at=R1 + 69632 + i * 4096)
               for i in range(2)]
        t45g2 = sba("t45g2", [128, 4 * GT], F32, at=R1 + 77824)

    pb = [nc.alloc_psum_tensor(f"pb{i}", [128, GT], F32) for i in range(8)]
    vT_ps = pb[4][:, 0:64].bitcast(BF16)
    khT_ps = pb[4][:, 64:128].bitcast(BF16)
    pt_ps = pb[7][:, 0:128]
    ds_ps = pb[7][:, 128:256]

    SV = Sem(nc, "SV")
    SS = Sem(nc, "SS")
    ST = Sem(nc, "ST")
    SG = Sem(nc, "SG")
    SY = Sem(nc, "SY")
    SC = Sem(nc, "SC")
    SB = Sem(nc, "SB")
    SD = Sem(nc, "SD")
    SD2 = Sem(nc, "SD2")
    SD3 = Sem(nc, "SD3")

    PH0_NLOAD = 8

    blk = nc.Block()
    block = blk.__enter__()

    @block.sync
    def _(sync):
        q = Q(sync, SY, dma=True)
        q.dma_start(out=idn_s[:], in_=c_idn)
        q.dma_start(out=idnf_s[:], in_=c_idnf)
        q.dma_start(out=mask_s[:], in_=c_mask)
        q.dma_start(out=ones1_s[:], in_=c_ones1)
        q.dma_start(out=onesk_s[:], in_=c_onesk)
        q.dma_start(out=oneskb_s[:], in_=c_oneskb)
        q.dma_start(out=gw_s[:], in_=c_gw)
        q.dma_start(out=xg_s[0][:].rearrange("p (j t) -> p j t", j=NJ),
                    in_=xT[:, 0:GT].rearrange("(j p) t -> p j t", p=128))
    ph0_sy = SY.n

    @block.gpsimd
    def _(gpsimd):
        q = Q(gpsimd, SG, dma=True)
        for w_hbm, w_sb in ((wqT, wq_s), (wfT, wf_s), (wiT, wi_s), (woT, wo_s)):
            q.dma_start(out=w_sb[:].rearrange("p (j o) -> p j o", j=NJ),
                        in_=w_hbm.rearrange("(j p) o -> p j o", p=128))
    ph0_g = SG.n

    @block.scalar
    def _(scalar):
        _load_act_set(nc, scalar)

    @block.vector
    def _(vector):
        q = Q(vector, SV)
        q.memset(st_f[:], 0.0)
        q.memset(st_b[:], 0.0)
        q.memset(x2max_s[:], 0.0)
        q.memset(gmU_s[:], 0.0)
        q.memset(gmG_s[:], 0.0)
        q.memset(gmT4_s[:], 0.0)
        q.memset(gmT5_s[:], 0.0)
        q.memset(vec_s[:, 3:4], -LN65536)
        q.memset(vec_s[:, 9:10], 1.0 / 65536.0)
        for i in range(8):
            q.memset(pb[i][:], 0.0)

    f1_ss = 0
    deq_done_c = 0
    if full:
        deq_units = ([(wdq, wloc, u) for u in range(UPC)]
                     + [(wdq2, wdloc, k) for k in range(WUPC)])
        for ui, (dq_src, dq_dst, u) in enumerate(deq_units):
            @block.sync
            def _(sync, ui=ui, dq_src=dq_src, u=u):
                q = Q(sync, SY, dma=True)
                if ui > 0:
                    q.wait(SV)
                q.dma_start(out=wraw_s[0][:],
                            in_=dq_src[u * 128:(u + 1) * 128, :])

            @block.vector
            def _(vector, ui=ui):
                q = Q(vector, SV)
                q.wait(SY)
                w3 = wraw_s[0][:].rearrange("p (b i) -> p b i", b=32)
                q.reduce_max(out=dqm_s[:], in_=w3, axis=AX.X,
                             apply_absolute_value=True)
                q.tensor_scalar(out=dqm_s[:], in0=dqm_s[:], scalar1=1.0 / 7.0,
                                scalar2=1e-12, op0=ALU.mult, op1=ALU.max)
                q.reciprocal(out=dqr_s[:], in_=dqm_s[:])
                rb = dqr_s[:].rearrange("p (b one) -> p b one",
                                        one=1).broadcast_to([128, 32, 32])
                q.tensor_tensor(out=dqt_s[:].rearrange("p (b i) -> p b i",
                                                       b=32),
                                in0=w3, in1=rb, op=ALU.mult)

            @block.vector
            def _(vector, ui=ui):
                q = Q(vector, SV)
                q.wait(ST)
                q.tensor_copy(out=dqi_s[:], in_=dqt_s[:])
                sb3 = dqm_s[:].rearrange("p (b one) -> p b one",
                                         one=1).broadcast_to([128, 32, 32])
                q.scalar_tensor_tensor(
                    out=dqb_s[:].rearrange("p (b i) -> p b i", b=32),
                    in0=dqi_s[:].rearrange("p (b i) -> p b i", b=32),
                    scalar=0.0, in1=sb3, op0=ALU.add, op1=ALU.mult)

            tq0 = ST.n
            sq0 = SS.n
            gq0 = SG.n

            @block.tensor
            def _(tensor, sq0=sq0, ui=ui, gq0=gq0):
                q = Q(tensor, ST, serialize=False)
                q.wait(SV)
                if sq0:
                    q.wait(SS, sq0)
                if ui > 0:
                    q.wait(SG, gq0)
                for j in range(NJ):
                    if j >= 2:
                        q.wait(SS, sq0 + (j - 1))
                    quarter = pb[6 + j % 2][:, 0:64].bitcast(BF16)
                    q.transpose(out=quarter,
                                in_=dqb_s[:, j * 128:(j + 1) * 128],
                                identity=idn_s[:])

            @block.scalar
            def _(scalar, tq0=tq0):
                q = Q(scalar, SS)
                for j in range(NJ):
                    q.wait(ST, tq0 + j + 1)
                    quarter = pb[6 + j % 2][:, 0:64].bitcast(BF16)
                    q.copy(out=dqo_s[:, j * 128:(j + 1) * 128], in_=quarter)

            @block.gpsimd
            def _(gpsimd, dq_dst=dq_dst, u=u):
                q = Q(gpsimd, SG, dma=True)
                q.wait(SS)
                q.dma_start(
                    out=dq_dst[u * 1024:(u + 1) * 1024, :].rearrange(
                        "(j p) o -> p j o", p=128),
                    in_=dqo_s[:].rearrange("p (j o) -> p j o", j=NJ))

        @block.gpsimd
        def _(gpsimd):
            gpsimd.wait_ge(SG.h, SG.n)
            for ins_t, outs_t in ((wloc, wgat), (wdloc, wdgat)):
                ins = gpsimd.collective_compute(
                    "AllGather", ALU.bypass,
                    replica_groups=[list(range(ncores))],
                    ins=[ins_t], outs=[outs_t])
                ins.then_inc(SC.h, 1)
                SC.n += 1
        deq_done_c = SC.n
        f1_ss = SS.n

    spill_sg = {}
    xload_sy = {0: ph0_sy}
    s1_carry = [0]
    v1_carry = [0]
    t2_carry = [0]
    s2_carry = [0]
    for g in range(NG):
        xg = xg_s[g % 2]
        gbase = g * GT

        t_pre = ST.n
        v_pre = SV.n

        @block.vector
        def _(vector, g=g, xg=xg, t_pre=t_pre):
            q = Q(vector, SV)
            if g > 0:
                q.wait(ST)
                if DBG:
                    q.wait(SG)
            if g % NG_SEQ == 0 and g > 0:
                q.memset(st_f[:], 0.0)
                q.memset(st_b[:], 0.0)
            q.wait(SY, xload_sy[g])
            if g == 0:
                q.wait(SG, ph0_g)
            for j in range(NJ):
                if j >= 2:
                    q.wait(ST, t_pre + 2 * (j - 1))
                xj = xg[:, j * GT:(j + 1) * GT]
                q.tensor_copy(out=pairx[j % 2][:], in_=xj)
                q.tensor_tensor(out=pairq[j % 2][:], in0=pairx[j % 2][:],
                                in1=pairx[j % 2][:], op=ALU.mult)

        @block.tensor
        def _(tensor, g=g, v_pre=v_pre):
            q = Q(tensor, ST, serialize=False)
            if g == 0 and f1_ss:
                q.wait(SS, f1_ss)
            for j in range(NJ):
                vbump = 2 if g % NG_SEQ == 0 and g > 0 else 0
                q.wait(SV, v_pre + vbump + 2 * (j + 1))
                q.matmul(out=pb[5][0:1, :], lhsT=oneskb_s[:], rhs=pairx[j % 2][:],
                         start=(j == 0), stop=(j == NJ - 1))
                q.matmul(out=pb[6][0:1, :], lhsT=oneskb_s[:], rhs=pairq[j % 2][:],
                         start=(j == 0), stop=(j == NJ - 1))

        @block.vector
        def _(vector, g=g):
            q = Q(vector, SV)
            q.wait(ST)
            q.tensor_scalar(out=mrow_s[:], in0=pb[5][0:1, :], scalar1=1.0 / D,
                            scalar2=None, op0=ALU.mult)
            q.tensor_scalar(out=vrow_s[:], in0=pb[6][0:1, :], scalar1=1.0 / D,
                            scalar2=EPS, op0=ALU.mult, op1=ALU.add)
            q.tensor_tensor(out=srow_s[:], in0=mrow_s[:], in1=mrow_s[:],
                            op=ALU.mult)
            q.tensor_tensor(out=vrow_s[:], in0=vrow_s[:], in1=srow_s[:],
                            op=ALU.subtract)

        @block.scalar
        def _(scalar, g=g):
            q = Q(scalar, SS)
            q.wait(SV)
            q.activation(out=vrow_s[:], in_=vrow_s[:], func=AF.Ln)
            q.activation(out=srow_s[:], in_=vrow_s[:], func=AF.Exp, scale=-0.5)

        @block.vector
        def _(vector, g=g):
            q = Q(vector, SV)
            q.wait(SS)
            q.tensor_tensor(out=mrow_s[:], in0=mrow_s[:], in1=srow_s[:],
                            op=ALU.mult)

        @block.tensor
        def _(tensor, g=g):
            q = Q(tensor, ST, serialize=False)
            q.wait(SV)
            q.matmul(out=pb[5][:], lhsT=ones1_s[:], rhs=srow_s[:],
                     start=True, stop=True)
            q.matmul(out=pb[6][:], lhsT=ones1_s[:], rhs=mrow_s[:],
                     start=True, stop=True)

        @block.vector
        def _(vector, g=g, xg=xg):
            q = Q(vector, SV)
            q.wait(ST)
            q.tensor_copy(out=r1bc_s[:], in_=pb[5][:])
            q.tensor_copy(out=mr1bc_s[:], in_=pb[6][:])
            for j in range(NJ):
                xj = xg[:, j * GT:(j + 1) * GT]
                q.tensor_tensor(out=tmp1_s[:], in0=xj, in1=r1bc_s[:],
                                op=ALU.mult)
                q.tensor_tensor(out=xn_s[:, j * GT:(j + 1) * GT], in0=tmp1_s[:],
                                in1=mr1bc_s[:], op=ALU.subtract)
        xn_done = SV.n

        if DBG == "xg":
            @block.gpsimd
            def _(gpsimd, g=g, gbase=gbase, xg=xg):
                q = Q(gpsimd, SG, dma=True)
                q.wait(SY, 16 * 7 + 16 * g)
                q.dma_start(out=outT[:, gbase:gbase + GT].rearrange(
                    "(j p) t -> p j t", p=128),
                    in_=xg[:].rearrange("p (j t) -> p j t", j=NJ))

        if DBG == "sums":
            @block.gpsimd
            def _(gpsimd, g=g, gbase=gbase):
                q = Q(gpsimd, SG, dma=True)
                q.wait(SV)
                q.dma_start(out=outT[0:1, gbase:gbase + GT], in_=mrow_s[:])
                q.dma_start(out=outT[1:2, gbase:gbase + GT], in_=vrow_s[:])
                q.dma_start(out=outT[2:3, gbase:gbase + GT], in_=srow_s[:])

        if DBG == "xn":
            @block.gpsimd
            def _(gpsimd, g=g, gbase=gbase):
                q = Q(gpsimd, SG, dma=True)
                q.wait(SV, xn_done)
                q.dma_start(out=outT[:, gbase:gbase + GT].rearrange(
                    "(j p) t -> p j t", p=128),
                    in_=xn_s[:].rearrange("p (j t) -> p j t", j=NJ))

        st_ready = SV.n

        def emit_t1(h, v1_prev, s1_prev):
            @block.tensor
            def _(tensor, h=h, xn_done=xn_done, v1_prev=v1_prev,
                  s1_prev=s1_prev):
                q = Q(tensor, ST, serialize=False)
                q.wait(SV, xn_done)
                if v1_prev:
                    q.wait(SV, v1_prev)
                if s1_prev:
                    q.wait(SS, s1_prev)
                for (wsb, bank) in ((wq_s, 0), (wf_s, 1), (wi_s, 2)):
                    for j in range(NJ):
                        q.matmul(
                            out=pb[bank][:],
                            lhsT=wsb[:, j * D + h * 128: j * D + (h + 1) * 128],
                            rhs=xn_s[:, j * GT:(j + 1) * GT],
                            start=(j == 0), stop=(j == NJ - 1))
            return ST.n

        t1_end = emit_t1(0, 0, 0)
        for h in range(H):
            @block.vector
            def _(vector, g=g, h=h, t1_end=t1_end):
                q = Q(vector, SV)
                if DBG:
                    q.wait(SG)
                q.wait(ST, t1_end)
                if s2_carry[0]:
                    q.wait(SS, s2_carry[0])
                if t2_carry[0]:
                    q.wait(ST, t2_carry[0])
                q.tensor_copy(out=qp_s[:], in_=pb[0][:])
                q.tensor_copy(out=uqt_s[:], in_=pb[1][:])
                q.tensor_copy(out=vb_s[:], in_=pb[2][:])
            v1_mid = SV.n

            @block.scalar
            def _(scalar, g=g, h=h, v1_mid=v1_mid, v1_prev=v1_carry[0]):
                q = Q(scalar, SS)
                q.wait(SV, v1_mid)
                if v1_prev:
                    q.wait(SV, v1_prev)
                q.activation(out=lq_s[:], in_=qp_s[:], func=AF.Exp, scale=-1.0)
                q.activation(out=lq_s[:], in_=lq_s[:], func=AF.Ln, bias=1.0)
                q.activation(out=lf_s[:], in_=uqt_s[:], func=AF.Exp,
                             scale=-1.0)
                q.activation(out=lf_s[:], in_=lf_s[:], func=AF.Ln, bias=1.0)
            s1_end = SS.n
            s1_carry[0] = s1_end

            @block.vector
            def _(vector, g=g, h=h, s1_end=s1_end):
                q = Q(vector, SV)
                q.wait(SS, s1_end)
                q.tensor_tensor_scan(out=lt_s[:], data0=onesk_s[:],
                                     data1=lf_s[:], initial=0.0,
                                     op0=ALU.mult, op1=ALU.add)
                q.tensor_scalar(out=tk_s[:, 0:1], in0=uqt_s[:, 0:1],
                                scalar1=-1.0, scalar2=None, op0=ALU.mult)
                q.tensor_tensor(out=tk_s[:, 1:GT], in0=lt_s[:, 0:GT - 1],
                                in1=uqt_s[:, 1:GT], op=ALU.subtract)
                q.tensor_tensor(out=uqt_s[:], in0=lq_s[:], in1=lt_s[:],
                                op=ALU.add)
                lt127 = lt_s[:, 127::128]
                lt63 = lt_s[:, 63::128]
                q.memset(bias_s[:, 0:1], LNC)
                q.tensor_scalar(out=bias_s[:, 1:4], in0=lt127[:, 0:3],
                                scalar1=1.0, scalar2=LNC, op0=ALU.mult,
                                op1=ALU.add)
                q.tensor_scalar(out=bias_s[:, 4:8], in0=lt63, scalar1=1.0,
                                scalar2=LNC, op0=ALU.mult, op1=ALU.add)
                q.memset(bias_s[:, 8:9], 0.0)
                q.tensor_scalar(out=bias_s[:, 9:12], in0=lt127[:, 0:3],
                                scalar1=-1.0, scalar2=None, op0=ALU.mult)
                q.tensor_scalar(out=bias_s[:, 12:16], in0=lt63, scalar1=-1.0,
                                scalar2=None, op0=ALU.mult)
                q.tensor_scalar(out=bias_s[:, 16:20], in0=lt127, scalar1=-1.0,
                                scalar2=None, op0=ALU.mult)
                q.tensor_copy(out=bias_s[:, 20:21], in_=bias_s[:, 16:17])
                q.tensor_tensor(out=bias_s[:, 21:24], in0=lt127[:, 0:3],
                                in1=lt127[:, 1:4], op=ALU.subtract)
            v1_end = SV.n

            if DBG in ("lf", "lt", "tk", "uqt", "lq") and h == 0:
                buf = {"lf": lf_s, "lt": lt_s, "tk": tk_s, "uqt": uqt_s,
                       "lq": lq_s}[DBG]

                @block.gpsimd
                def _(gpsimd, g=g, buf=buf, v1_end=v1_end):
                    q = Q(gpsimd, SG, dma=True)
                    q.wait(SV, v1_end)
                    q.dma_start(out=outT[0:128, g * GT:(g + 1) * GT],
                                in_=buf[:])

            v1_carry[0] = v1_end
            t1_next = emit_t1(h + 1, v1_mid, 0) if h + 1 < H else None
            s2_pre = SS.n
            t2_pre = ST.n
            v2_pre = SV.n
            NS2 = 8
            NT2 = 8
            NV2 = 5

            @block.scalar
            def _(scalar, g=g, h=h, v1_end=v1_end, v2_pre=v2_pre,
                  t2_pre=t2_pre, t2_prev=t2_carry[0]):
                q = Q(scalar, SS)
                for c in range(NCH):
                    cs = c * CH
                    q.wait(SV, v1_end)
                    if c == 0 and t2_prev:
                        q.wait(ST, t2_prev)
                    if c > 0:
                        q.wait(SV, v2_pre + NV2 * (c - 1) + 2)
                        q.wait(ST, t2_pre + NT2 * (c - 1) + 5)
                    q.activation(out=eqa_s[:], in_=uqt_s[:, cs:cs + CH],
                                 func=AF.Exp, scale=-1.0,
                                 bias=bias_s[:, c:c + 1])
                    q.activation(out=eqb_s[:], in_=uqt_s[:, cs + 64:cs + CH],
                                 func=AF.Exp, scale=-1.0,
                                 bias=bias_s[:, 4 + c:5 + c])
                    q.activation(out=kta_s[:], in_=tk_s[:, cs:cs + 64],
                                 func=AF.Exp, bias=bias_s[:, 8 + c:9 + c])
                    q.activation(out=ktb_s[:], in_=tk_s[:, cs:cs + CH],
                                 func=AF.Exp, bias=bias_s[:, 12 + c:13 + c])
                    q.activation(out=khat_s[:], in_=tk_s[:, cs:cs + CH],
                                 func=AF.Exp, bias=bias_s[:, 16 + c:17 + c])
                    q.activation(out=bias_s[:, 24 + c:25 + c],
                                 in_=bias_s[:, 20 + c:21 + c], func=AF.Exp)
                    q.wait(ST, t2_pre + NT2 * c + 2)
                    q.copy(out=vT_s[:], in_=vT_ps)
                    q.copy(out=khT_s[:], in_=khT_ps)
            s2_end = SS.n
            s2_carry[0] = s2_end

            @block.tensor
            def _(tensor, g=g, h=h, v1_end=v1_end, s2_pre=s2_pre,
                  v2_pre=v2_pre, st_ready=st_ready):
                q = Q(tensor, ST, serialize=False)
                for c in range(NCH):
                    cs = c * CH
                    q.wait(SV, v1_end)
                    q.wait(SS, s2_pre + NS2 * c + 5)
                    q.transpose(out=vT_ps, in_=vb_s[:, cs:cs + CH],
                                identity=idn_s[:])
                    q.transpose(out=khT_ps, in_=khat_s[:], identity=idn_s[:])
                    q.wait(SV, v2_pre + NV2 * c + 2)
                    q.matmul(out=pt_ps[0:64, 0:64], lhsT=kta_s[:],
                             rhs=qta_s[:, 0:64], start=True, stop=True)
                    q.matmul(out=pt_ps[0:64, 64:128], lhsT=ktb_s[:, 0:64],
                             rhs=qtb_s[:], start=True, stop=True)
                    q.matmul(out=pt_ps[64:128, 64:128], lhsT=ktb_s[:, 64:128],
                             rhs=qtb_s[:], start=True, stop=True)
                    q.wait(SV, v2_pre + NV2 * c + 3)
                    if c == 0:
                        q.wait(SV, st_ready)
                    q.wait(SS, s2_pre + NS2 * c + 8)
                    otq = pb[3][:, cs:cs + CH]
                    q.matmul(out=otq, lhsT=vT_s[:], rhs=pt_s[:],
                             start=True, stop=False)
                    q.matmul(out=otq, lhsT=st_b[:, h * 128:(h + 1) * 128],
                             rhs=qta_s[:], start=False, stop=True)
                    q.matmul(out=ds_ps, lhsT=khT_s[:], rhs=vT_s[:],
                             start=True, stop=True)
            t2_end = ST.n
            t2_carry[0] = t2_end

            @block.vector
            def _(vector, g=g, h=h, s2_pre=s2_pre, t2_pre=t2_pre):
                q = Q(vector, SV)
                for c in range(NCH):
                    cs = c * CH
                    q.wait(SS, s2_pre + NS2 * c + 6)
                    q.tensor_tensor(out=qta_s[:], in0=qp_s[:, cs:cs + CH],
                                    in1=eqa_s[:], op=ALU.mult)
                    q.tensor_tensor(out=qtb_s[:], in0=qp_s[:, cs + 64:cs + CH],
                                    in1=eqb_s[:], op=ALU.mult)
                    q.wait(ST, t2_pre + NT2 * c + 5)
                    q.scalar_tensor_tensor(out=pt_s[:], in0=pt_ps, scalar=1.0,
                                           in1=mask_s[:], op0=ALU.mult,
                                           op1=ALU.mult)
                    q.wait(ST, t2_pre + NT2 * c + 8)
                    sl = st_f[:, h * 128:(h + 1) * 128]
                    q.scalar_tensor_tensor(out=sl, in0=sl,
                                           scalar=bias_s[:, 24 + c:25 + c],
                                           in1=ds_ps, op0=ALU.mult, op1=ALU.add)
                    q.tensor_copy(out=st_b[:, h * 128:(h + 1) * 128], in_=sl)
                q.wait(ST, t2_pre + NT2 * NCH)
                q.tensor_copy(out=of_s[:, h * GT:(h + 1) * GT], in_=pb[3][:])
            st_ready = SV.n
            t1_end = t1_next

        if DBG == "of":
            @block.gpsimd
            def _(gpsimd, g=g, gbase=gbase):
                q = Q(gpsimd, SG, dma=True)
                q.wait(SV)
                q.dma_start(out=outT[:, gbase:gbase + GT].rearrange(
                    "(j p) t -> p j t", p=128),
                    in_=of_s[:].rearrange("p (j t) -> p j t", j=NJ))

        tr_pre = ST.n
        vr_pre = SV.n

        @block.vector
        def _(vector, g=g, tr_pre=tr_pre):
            q = Q(vector, SV)
            for j in range(NJ):
                if j >= 2:
                    q.wait(ST, tr_pre + (j - 1))
                ofj = of_s[:, j * GT:(j + 1) * GT]
                q.tensor_tensor(out=pairx[j % 2][:], in0=ofj, in1=ofj,
                                op=ALU.mult)

        @block.tensor
        def _(tensor, g=g, vr_pre=vr_pre):
            q = Q(tensor, ST, serialize=False)
            for j in range(NJ):
                q.wait(SV, vr_pre + j + 1)
                q.matmul(out=pb[1][0:1, :], lhsT=oneskb_s[:], rhs=pairx[j % 2][:],
                         start=(j == 0), stop=(j == NJ - 1))

        @block.vector
        def _(vector, g=g):
            q = Q(vector, SV)
            q.wait(ST)
            q.tensor_scalar(out=srow_s[:], in0=pb[1][0:1, :], scalar1=1.0 / D,
                            scalar2=EPS, op0=ALU.mult, op1=ALU.add)

        @block.scalar
        def _(scalar, g=g):
            q = Q(scalar, SS)
            q.wait(SV)
            q.activation(out=srow_s[:], in_=srow_s[:], func=AF.Ln)
            q.activation(out=srow_s[:], in_=srow_s[:], func=AF.Exp, scale=-0.5)

        @block.tensor
        def _(tensor, g=g):
            q = Q(tensor, ST, serialize=False)
            q.wait(SS)
            q.matmul(out=pb[2][:], lhsT=ones1_s[:], rhs=srow_s[:],
                     start=True, stop=True)

        @block.vector
        def _(vector, g=g):
            q = Q(vector, SV)
            q.wait(ST)
            q.tensor_copy(out=r1bc_s[:], in_=pb[2][:])
            for j in range(NJ):
                q.tensor_tensor(out=of_s[:, j * GT:(j + 1) * GT],
                                in0=of_s[:, j * GT:(j + 1) * GT],
                                in1=r1bc_s[:], op=ALU.mult)
        vofn_end = SV.n

        @block.scalar
        def _(scalar, g=g, vofn_end=vofn_end):
            q = Q(scalar, SS)
            for j in range(NJ):
                q.wait(SV, vofn_end - NJ + j + 1)
                q.activation(out=ofb_s[:, j * GT:(j + 1) * GT],
                             in_=of_s[:, j * GT:(j + 1) * GT],
                             func=AF.Copy, scale=gw_s[:, j:j + 1])
        sofb_end = SS.n

        vres_pre = SV.n

        @block.tensor
        def _(tensor, g=g, sofb_end=sofb_end, vres_pre=vres_pre):
            q = Q(tensor, ST, serialize=False)
            q.wait(SS, sofb_end)
            for ot in range(NJ):
                if ot >= 2:
                    q.wait(SV, vres_pre + (ot - 1))
                bank = pb[5] if ot % 2 == 0 else pb[6]
                for j in range(NJ):
                    q.matmul(
                        out=bank[:],
                        lhsT=wo_s[:, j * D + ot * 128: j * D + (ot + 1) * 128],
                        rhs=ofb_s[:, j * GT:(j + 1) * GT],
                        start=(j == 0), stop=(j == NJ - 1))
        two_pre = ST.n - NJ * NJ

        @block.vector
        def _(vector, g=g, xg=xg, two_pre=two_pre):
            q = Q(vector, SV)
            if DBG:
                q.wait(SG)
            for ot in range(NJ):
                q.wait(ST, two_pre + NJ * (ot + 1))
                bank = pb[5] if ot % 2 == 0 else pb[6]
                q.scalar_tensor_tensor(out=xg[:, ot * GT:(ot + 1) * GT],
                                       in0=bank[:], scalar=1.0,
                                       in1=xg[:, ot * GT:(ot + 1) * GT],
                                       op0=ALU.mult, op1=ALU.add)
        vxp_end = SV.n

        @block.gpsimd
        def _(gpsimd, g=g, gbase=gbase, vxp_end=vxp_end, xg=xg):
            q = Q(gpsimd, SG, dma=True)
            q.wait(SV, vxp_end)
            q.dma_start(
                out=xp_sp[:, gbase:gbase + GT].rearrange("(j p) t -> p j t",
                                                         p=128),
                in_=xg[:].rearrange("p (j t) -> p j t", j=NJ))
        spill_sg[g] = SG.n

        if g + 1 < NG:
            @block.sync
            def _(sync, g=g, vxp_end=vxp_end):
                q = Q(sync, SY, dma=True)
                q.wait(SV, vxp_end)
                if g - 1 in spill_sg:
                    q.wait(SG, spill_sg[g - 1])
                q.dma_start(
                    out=xg_s[(g + 1) % 2][:].rearrange("p (j t) -> p j t", j=NJ),
                    in_=xT[:, (g + 1) * GT:(g + 2) * GT].rearrange(
                        "(j p) t -> p j t", p=128))
                xload_sy[g + 1] = SY.n

        t_pre2 = ST.n
        v_pre2 = SV.n

        @block.vector
        def _(vector, g=g, t_pre2=t_pre2, xg=xg):
            q = Q(vector, SV)
            q.wait(ST)
            for j in range(NJ):
                if j >= 2:
                    q.wait(ST, t_pre2 + 2 * (j - 1))
                xj = xg[:, j * GT:(j + 1) * GT]
                q.tensor_copy(out=pairx[j % 2][:], in_=xj)
                q.tensor_tensor(out=pairq[j % 2][:], in0=pairx[j % 2][:],
                                in1=pairx[j % 2][:], op=ALU.mult)

        @block.tensor
        def _(tensor, g=g, v_pre2=v_pre2):
            q = Q(tensor, ST, serialize=False)
            for j in range(NJ):
                q.wait(SV, v_pre2 + 2 * (j + 1))
                q.matmul(out=pb[5][0:1, :], lhsT=oneskb_s[:], rhs=pairx[j % 2][:],
                         start=(j == 0), stop=(j == NJ - 1))
                q.matmul(out=pb[6][0:1, :], lhsT=oneskb_s[:], rhs=pairq[j % 2][:],
                         start=(j == 0), stop=(j == NJ - 1))

        @block.vector
        def _(vector, g=g):
            q = Q(vector, SV)
            q.wait(ST)
            q.tensor_scalar(out=mrow_s[:], in0=pb[5][0:1, :], scalar1=1.0 / D,
                            scalar2=None, op0=ALU.mult)
            q.tensor_scalar(out=vrow_s[:], in0=pb[6][0:1, :], scalar1=1.0 / D,
                            scalar2=EPS, op0=ALU.mult, op1=ALU.add)
            q.tensor_tensor(out=srow_s[:], in0=mrow_s[:], in1=mrow_s[:],
                            op=ALU.mult)
            q.tensor_tensor(out=vrow_s[:], in0=vrow_s[:], in1=srow_s[:],
                            op=ALU.subtract)

        @block.scalar
        def _(scalar, g=g, gbase=gbase):
            q = Q(scalar, SS)
            q.wait(SV)
            q.activation(out=vrow_s[:], in_=vrow_s[:], func=AF.Ln)
            q.activation(out=r2row_s[0:1, gbase:gbase + GT], in_=vrow_s[:],
                         func=AF.Exp, scale=-0.5)

        @block.vector
        def _(vector, g=g, gbase=gbase):
            q = Q(vector, SV)
            q.wait(SS)
            q.tensor_tensor(out=mr2row_s[0:1, gbase:gbase + GT], in0=mrow_s[:],
                            in1=r2row_s[0:1, gbase:gbase + GT], op=ALU.mult)

        @block.tensor
        def _(tensor, g=g, gbase=gbase):
            q = Q(tensor, ST, serialize=False)
            q.wait(SV)
            q.matmul(out=pb[5][:], lhsT=ones1_s[:],
                     rhs=r2row_s[0:1, gbase:gbase + GT], start=True, stop=True)
            q.matmul(out=pb[6][:], lhsT=ones1_s[:],
                     rhs=mr2row_s[0:1, gbase:gbase + GT], start=True, stop=True)

        @block.vector
        def _(vector, g=g, xg=xg):
            q = Q(vector, SV)
            q.wait(ST)
            for j in range(NJ):
                xj = xg[:, j * GT:(j + 1) * GT]
                q.tensor_tensor(out=tmp1_s[:], in0=xj, in1=pb[5][:],
                                op=ALU.mult)
                q.tensor_tensor(out=tmp1_s[:], in0=tmp1_s[:], in1=pb[6][:],
                                op=ALU.subtract)
                q.reduce_max(out=redt_s[:], in_=tmp1_s[:], axis=AX.X,
                             apply_absolute_value=True)
                q.tensor_tensor(out=x2max_s[:], in0=x2max_s[:], in1=redt_s[:],
                                op=ALU.max)

    attn_v = SV.n
    attn_t = ST.n
    attn_s = SS.n
    attn_g = SG.n
    attn_y = SY.n

    if stage == "attn":
        @block.gpsimd
        def _(gpsimd):
            q = Q(gpsimd, SG, dma=True)
            q.wait(SG, attn_g)
            for g in range(NG):
                q.dma_start(out=outT[:, g * GT:(g + 1) * GT],
                            in_=xp_sp[:, g * GT:(g + 1) * GT])
            gpsimd.wait_ge(SG.h, SG.n)
        blk.__exit__(None, None, None)
        return nc

    NGB = NOB // 4
    HALF = 65536.0

    def allreduce(idx, gm_col, pre_v, post):
        @block.tensor
        def _(tensor):
            q = Q(tensor, ST, serialize=False)
            q.wait(SV, max(pre_v, SV.n))
            q.matmul(out=pb[6][0:1, 0:128], lhsT=gm_col[:], rhs=idnf_s[:],
                     start=True, stop=True)

        @block.vector
        def _(vector):
            q = Q(vector, SV)
            q.wait(ST)
            q.reduce_max(out=sc_s[0:1, L0:L0 + 1], in_=pb[6][0:1, 0:128],
                         axis=AX.X)

        @block.gpsimd
        def _(gpsimd):
            q = Q(gpsimd, SG, dma=True)
            q.wait(SV)
            q.dma_start(out=cc_in[idx], in_=sc_s[0:1, L0:L0 + 1])
            gpsimd.wait_ge(SG.h, SG.n)
            ins = gpsimd.collective_compute(
                "AllReduce", ALU.max,
                replica_groups=[list(range(ncores))],
                ins=[cc_in[idx]], outs=[cc_out[idx]])
            ins.then_inc(SC.h, 1)
            SC.n += 1
            gpsimd.wait_ge(SC.h, SC.n)
            q.dma_start(out=sc_s[0:1, L1:L1 + 1], in_=cc_out[idx])

        @block.vector
        def _(vector):
            q = Q(vector, SV)
            q.wait(SG)
            post(q)

    def bcast_vec(src_lo, n, dst_lo):
        @block.tensor
        def _(tensor):
            q = Q(tensor, ST, serialize=False)
            q.wait(SV)
            q.matmul(out=pb[6][:, 0:n], lhsT=ones1_s[:],
                     rhs=sc_s[0:1, src_lo:src_lo + n], start=True, stop=True)

        @block.vector
        def _(vector):
            q = Q(vector, SV)
            q.wait(ST)
            q.tensor_copy(out=vec_s[:, dst_lo:dst_lo + n],
                          in_=pb[6][:, 0:n])

    L0, L1 = 0, 1

    def post_s1(q):
        q.tensor_scalar(out=sc_s[0:1, 2:3], in0=sc_s[0:1, 1:2],
                        scalar1=1.0 / 32767.0, scalar2=1e-12,
                        op0=ALU.mult, op1=ALU.max)
        q.reciprocal(out=sc_s[0:1, 3:4], in_=sc_s[0:1, 2:3])
        q.tensor_scalar(out=sc_s[0:1, 4:5], in0=sc_s[0:1, 3:4],
                        scalar1=-1.0, scalar2=None, op0=ALU.mult)
        q.tensor_scalar(out=r2row_s[:], in0=r2row_s[:],
                        scalar1=sc_s[0:1, 3:4], scalar2=None, op0=ALU.mult)
        q.tensor_scalar(out=mr2row_s[:], in0=mr2row_s[:],
                        scalar1=sc_s[0:1, 4:5], scalar2=None, op0=ALU.mult)

    allreduce(0, x2max_s, attn_v, post_s1)


    f2_bc, f2_p, f2_v, f2_sb = {}, {}, {}, {}
    for tt in range(NG):
        @block.sync
        def _(sync, tt=tt):
            q = Q(sync, SY, dma=True)
            if tt == 0:
                q.wait(ST, attn_t)
                q.wait(SG, attn_g)
            else:
                q.wait(SB, f2_sb[tt - 1])
            q.dma_start(out=xpr_s[:].rearrange("p (j t) -> p j t", j=NJ),
                        in_=xp_sp[:, tt * GT:(tt + 1) * GT].rearrange(
                            "(j p) t -> p j t", p=128))

        @block.tensor
        def _(tensor, tt=tt):
            q = Q(tensor, ST, serialize=False)
            q.wait(SV)
            q.matmul(out=pb[4][:], lhsT=ones1_s[:],
                     rhs=r2row_s[0:1, tt * GT:(tt + 1) * GT],
                     start=True, stop=True)
            q.matmul(out=pb[5][:], lhsT=ones1_s[:],
                     rhs=mr2row_s[0:1, tt * GT:(tt + 1) * GT],
                     start=True, stop=True)

        @block.vector
        def _(vector, tt=tt):
            q = Q(vector, SV)
            q.wait(SY)
            q.wait(ST)
            if tt > 0:
                q.wait(SB, f2_sb[tt - 1])
            q.tensor_copy(out=r1bc_s[:], in_=pb[4][:])
            q.tensor_copy(out=mr1bc_s[:], in_=pb[5][:])
        f2_bc[tt] = SV.n

        for j in range(NJ):
            jj = 8 * tt + j

            @block.gpsimd
            def _(gpsimd, tt=tt, j=j, jj=jj):
                q = Q(gpsimd, SB)
                if j == 0:
                    q.wait(SV, f2_bc[tt])
                if jj >= 2:
                    q.wait(SV, f2_v[jj - 2])
                q.tensor_tensor(out=xf2[j % 2][:],
                                in0=xpr_s[:, j * GT:(j + 1) * GT],
                                in1=r1bc_s[:], op=ALU.mult)
                f2_p[jj] = SB.n

            @block.vector
            def _(vector, tt=tt, j=j, jj=jj):
                q = Q(vector, SV)
                q.wait(SB, f2_p[jj])
                q.scalar_tensor_tensor(out=xfi[:], in0=xf2[j % 2][:],
                                       scalar=1.0, in1=mr1bc_s[:],
                                       op0=ALU.mult, op1=ALU.add)
                q.tensor_copy(out=x1b_s[:, j * TOK + tt * GT:
                                        j * TOK + (tt + 1) * GT],
                              in_=xfi[:])
                f2_v[jj] = SV.n
        f2_sb[tt] = SB.n
    x1_done_v = SV.n

    NGT = NG // 4

    def big_mm_phase(ubase, dst_sp, gm_col, first, ii=None):
        t0 = ST.n
        y_wt = {}
        y_g = {}
        evac_ss = {}
        spill_sg = {}
        iispill_sg = {}
        vdone = {}
        v1_done = {}
        v3_done = {}
        s_done = {}

        def ii_rd(tg):
            grp, tile = divmod(tg, 2)
            return rd_s[grp % 4][:, tile * 1024:(tile + 1) * 1024]

        def ii_t45a(tg):
            grp, tile = divmod(tg, 2)
            return t45a_s[grp % 2][:, tile * 1024:(tile + 1) * 1024]

        @block.sync
        def _(sync):
            q = Q(sync, SY, dma=True)
            if first:
                q.wait(SC, deq_done_c)
            q.wait(ST)
            q.dma_start(
                out=wt_s[0][:].rearrange("p (j o) -> p j o", j=NJ),
                in_=wgat[ubase * 1024:(ubase + 1) * 1024, :].rearrange(
                    "(j p) o -> p j o", p=128))
            y_wt[0] = SY.n
            if ii:
                q.wait(SG)
                for grp in range(4):
                    ob0, ttg0 = divmod(grp, NGT)
                    q.dma_start(out=rd_s[grp % 4][:],
                                in_=ii["src"][ob0 * 128:(ob0 + 1) * 128,
                                              ttg0 * 4 * GT:(ttg0 + 1) * 4 * GT])
                    y_g[grp] = SY.n

        for ob in range(NOB):
            @block.sync
            def _(sync, ob=ob):
                q = Q(sync, SY, dma=True)
                if ob + 1 < NOB:
                    q.wait(ST, t0 + 64 * ob)
                    q.dma_start(
                        out=wt_s[(ob + 1) % 2][:].rearrange("p (j o) -> p j o",
                                                            j=NJ),
                        in_=wgat[(ubase + ob + 1) * 1024:
                                 (ubase + ob + 2) * 1024, :].rearrange(
                            "(j p) o -> p j o", p=128))
                    y_wt[ob + 1] = SY.n

            @block.tensor
            def _(tensor, ob=ob):
                q = Q(tensor, ST, serialize=False)
                if ob == 0:
                    q.wait(SV)
                q.wait(SY, y_wt[ob])
                for half in range(2):
                    if ob > 0:
                        q.wait(SS, evac_ss[(ob - 1, half)])
                    for j in range(NJ):
                        for t4 in range(4):
                            tt = half * 4 + t4
                            q.matmul(
                                out=pb[tt][:],
                                lhsT=wt_s[ob % 2][:, j * 128:(j + 1) * 128],
                                rhs=x1b_s[:, j * TOK + tt * GT:
                                          j * TOK + (tt + 1) * GT],
                                start=(j == 0), stop=(j == NJ - 1))

            def emit_evac(ob, half):
                @block.scalar
                def _(scalar, ob=ob, half=half):
                    q = Q(scalar, SS)
                    for t4 in range(4):
                        tt = half * 4 + t4
                        q.wait(ST, t0 + 64 * ob + 32 * half + 28 + t4 + 1)
                        if ob > 0:
                            q.wait(SG, spill_sg[(ob - 1, half)])
                            q.wait(SV, vdone[(ob - 1, half)])
                        q.copy(out=stg_s[half][:, t4 * GT:(t4 + 1) * GT],
                               in_=pb[tt][:])
                        if t4 == 3:
                            evac_ss[(ob, half)] = SS.n

            def ii_v23(q, tg):
                grp, tile = divmod(tg, 2)
                p = tg % 2
                if tile == 0 and grp >= 2:
                    q.wait(SG, iispill_sg[grp - 2])
                q.wait(SS, s_done[tg])
                q.tensor_copy(out=sib[p][:], in_=sfb[p][:])
                q.scalar_tensor_tensor(out=ii_t45a(tg), in0=gii[p][:],
                                       scalar=0.0, in1=sib[p][:],
                                       op0=ALU.add, op1=ALU.mult)
                v3_done[tg] = SV.n
                if tile == 1:
                    q.reduce_max(out=redt_s[:], in_=t45a_s[grp % 2][:],
                                 axis=AX.X, apply_absolute_value=True)
                    q.tensor_tensor(out=ii["gm"][:], in0=ii["gm"][:],
                                    in1=redt_s[:], op=ALU.max)

            def emit_ii_spill(gsp):
                @block.gpsimd
                def _(gpsimd, gsp=gsp):
                    q = Q(gpsimd, SG, dma=True)
                    q.wait(SV, v3_done[2 * gsp + 1])
                    ob0, ttg0 = divmod(gsp, NGT)
                    q.dma_start(
                        out=ii["dst"][ob0 * 128:(ob0 + 1) * 128,
                                      ttg0 * 4 * GT:(ttg0 + 1) * 4 * GT],
                        in_=t45a_s[gsp % 2][:])
                    iispill_sg[gsp] = SG.n

            if ii:
                for sub in range(2):
                    grp = 2 * ob + sub
                    for tile in range(2):
                        tg = 2 * grp + tile

                        @block.vector
                        def _(vector, tg=tg, grp=grp, tile=tile):
                            q = Q(vector, SV)
                            if tile == 0:
                                q.wait(SY, y_g[grp])
                            if tg >= 2:
                                q.wait(SS, s_done[tg - 2])
                            q.tensor_scalar(out=gii[tg % 2][:], in0=ii_rd(tg),
                                            scalar1=vec_s[:, 1:2],
                                            scalar2=None, op0=ALU.mult)
                            v1_done[tg] = SV.n
                            if tg >= 1:
                                ii_v23(q, tg - 1)

                        @block.scalar
                        def _(scalar, tg=tg):
                            q = Q(scalar, SS)
                            q.wait(SV, v1_done[tg])
                            if tg >= 2:
                                q.wait(SV, v3_done[tg - 2])
                            p = tg % 2
                            q.activation(out=sfb[p][:], in_=gii[p][:],
                                         func=AF.Exp, scale=vec_s[:, 2:3],
                                         bias=vec_s[:, 3:4])
                            q.activation(out=sfb[p][:], in_=sfb[p][:],
                                         func=AF.Ln, bias=vec_s[:, 9:10])
                            q.activation(out=sfb[p][:], in_=sfb[p][:],
                                         func=AF.Exp, scale=-1.0)
                            s_done[tg] = SS.n

                        if tg >= 1 and (tg - 1) % 2 == 1:
                            emit_ii_spill((tg - 1) // 2)

                    emit_evac(ob, sub)
            else:
                emit_evac(ob, 0)
                emit_evac(ob, 1)

            if ii:
                @block.sync
                def _(sync, ob=ob):
                    q = Q(sync, SY, dma=True)
                    for grp in (2 * ob + 4, 2 * ob + 5):
                        if grp < 2 * NOB:
                            q.wait(SV, v1_done[2 * (grp - 4) + 1])
                            ob0, ttg0 = divmod(grp, NGT)
                            q.dma_start(
                                out=rd_s[grp % 4][:],
                                in_=ii["src"][ob0 * 128:(ob0 + 1) * 128,
                                              ttg0 * 4 * GT:(ttg0 + 1) * 4 * GT])
                            y_g[grp] = SY.n

            @block.vector
            def _(vector, ob=ob):
                q = Q(vector, SV)
                for half in range(2):
                    q.wait(SS, evac_ss[(ob, half)])
                    q.reduce_max(out=redt_s[:], in_=stg_s[half][:], axis=AX.X,
                                 apply_absolute_value=True)
                    q.tensor_tensor(out=gm_col[:], in0=gm_col[:],
                                    in1=redt_s[:], op=ALU.max)
                    vdone[(ob, half)] = SV.n

            @block.gpsimd
            def _(gpsimd, ob=ob):
                q = Q(gpsimd, SG, dma=True)
                for half in range(2):
                    q.wait(SS, evac_ss[(ob, half)])
                    q.dma_start(
                        out=dst_sp[ob * 128:(ob + 1) * 128,
                                   half * 4 * GT:(half + 1) * 4 * GT],
                        in_=stg_s[half][:])
                    spill_sg[(ob, half)] = SG.n

        if ii:
            @block.vector
            def _(vector):
                q = Q(vector, SV)
                ii_v23(q, 4 * NOB - 1)
            emit_ii_spill(2 * NOB - 1)
        return SV.n

    gate_done_v = big_mm_phase(0, g_sp, gmG_s, True)

    def post_s3(q):
        q.tensor_scalar(out=sc_s[0:1, 7:8], in0=sc_s[0:1, 6:7],
                        scalar1=sc_s[0:1, 2:3], scalar2=None, op0=ALU.mult)
        q.tensor_scalar(out=sc_s[0:1, 8:9], in0=sc_s[0:1, 7:8],
                        scalar1=1.0 / 32767.0, scalar2=1e-12,
                        op0=ALU.mult, op1=ALU.max)
        q.reciprocal(out=sc_s[0:1, 9:10], in_=sc_s[0:1, 8:9])
        q.tensor_scalar(out=sc_s[0:1, 10:11], in0=sc_s[0:1, 9:10],
                        scalar1=sc_s[0:1, 2:3], scalar2=None, op0=ALU.mult)
        q.tensor_scalar(out=sc_s[0:1, 11:12], in0=sc_s[0:1, 8:9],
                        scalar1=-1.0, scalar2=None, op0=ALU.mult)
        q.tensor_scalar(out=sc_s[0:1, 22:23], in0=sc_s[0:1, 8:9],
                        scalar1=1.0 / HALF, scalar2=None, op0=ALU.mult)

    L0, L1 = 5, 6
    allreduce(1, gmG_s, gate_done_v, post_s3)
    bcast_vec(10, 2, 1)

    up_done_v = big_mm_phase(NOB, u_sp, gmU_s, False,
                             ii={"src": g_sp, "dst": t4_sp, "gm": gmT4_s})
    ii_done_v = up_done_v

    def post_s2(q):
        q.tensor_scalar(out=sc_s[0:1, 15:16], in0=sc_s[0:1, 14:15],
                        scalar1=sc_s[0:1, 2:3], scalar2=None, op0=ALU.mult)
        q.tensor_scalar(out=sc_s[0:1, 16:17], in0=sc_s[0:1, 15:16],
                        scalar1=1.0 / 32767.0, scalar2=1e-12,
                        op0=ALU.mult, op1=ALU.max)
        q.reciprocal(out=sc_s[0:1, 17:18], in_=sc_s[0:1, 16:17])
        q.tensor_scalar(out=sc_s[0:1, 18:19], in0=sc_s[0:1, 17:18],
                        scalar1=sc_s[0:1, 2:3], scalar2=None, op0=ALU.mult)

    L0, L1 = 13, 14
    allreduce(2, gmU_s, up_done_v, post_s2)
    bcast_vec(18, 1, 4)

    def stage_iii(gm_col):
        y0 = SY.n
        v0 = SV.n
        d4 = {}
        s_ui = {}
        v_stt = {}
        g_spill = {}
        ubufs = (rd_s[0], rd_s[2], stg_s[0], stg_s[1])
        tbufs = (rd_s[1], rd_s[3], t45a_s[0], t45a_s[1])
        t45w = (t45g_s[0], t45g_s[1], t45g2)

        for grp in range(NOB * NGT):
            ob, ttg = divmod(grp, NGT)

            @block.sync
            def _(sync, grp=grp, ob=ob, ttg=ttg):
                q = Q(sync, SY, dma=True)
                if grp == 0:
                    q.wait(SG)
                    q.wait(SV, v0)
                if grp >= 4:
                    q.wait(SS, s_ui[2 * (grp - 4) + 1])
                q.dma_start(out=ubufs[grp % 4][:],
                            in_=u_sp[ob * 128:(ob + 1) * 128,
                                     ttg * 4 * GT:(ttg + 1) * 4 * GT])

            if grp < 2:
                @block.gpsimd
                def _(gpsimd, grp=grp, ob=ob, ttg=ttg):
                    q = Q(gpsimd, SD3, dma=True)
                    if grp == 0:
                        q.wait(SG)
                        q.wait(SV, v0)
                    q.dma_start(out=tbufs[grp % 4][:],
                                in_=t4_sp[ob * 128:(ob + 1) * 128,
                                          ttg * 4 * GT:(ttg + 1) * 4 * GT])
                d4[grp] = SD3.n

            @block.scalar
            def _(scalar, grp=grp):
                q = Q(scalar, SS)
                if grp == 0:
                    q.wait(SV, v0)
                q.wait(SY, y0 + 16 * (grp + 1))
                for tile in range(2):
                    tg = 2 * grp + tile
                    if tg >= 2:
                        q.wait(SV, v_stt[tg - 2])
                    q.activation(
                        out=gii[tg % 2][:], func=AF.Copy,
                        in_=ubufs[grp % 4][:, tile * 1024:(tile + 1) * 1024],
                        scale=vec_s[:, 4:5])
                    s_ui[tg] = SS.n

            @block.vector
            def _(vector, grp=grp):
                q = Q(vector, SV)
                q.wait(SD3, d4[grp])
                for tile in range(2):
                    tg = 2 * grp + tile
                    q.tensor_scalar(
                        out=sib[tg % 2][:],
                        in0=tbufs[grp % 4][:, tile * 1024:(tile + 1) * 1024],
                        scalar1=vec_s[:, 5:6], scalar2=None, op0=ALU.mult)
                for tile in range(2):
                    tg = 2 * grp + tile
                    q.wait(SS, s_ui[tg])
                    if tile == 0 and grp >= 3:
                        q.wait(SG, g_spill[grp - 3])
                    q.scalar_tensor_tensor(
                        out=t45w[grp % 3][:, tile * 1024:(tile + 1) * 1024],
                        in0=gii[tg % 2][:], scalar=0.0, in1=sib[tg % 2][:],
                        op0=ALU.add, op1=ALU.mult)
                    v_stt[tg] = SV.n
                q.reduce_max(out=redt_s[:], in_=t45w[grp % 3][:], axis=AX.X,
                             apply_absolute_value=True)
                q.tensor_tensor(out=gm_col[:], in0=gm_col[:], in1=redt_s[:],
                                op=ALU.max)

            @block.gpsimd
            def _(gpsimd, grp=grp, ob=ob, ttg=ttg):
                if grp + 2 < NOB * NGT:
                    g2 = grp + 2
                    ob2, ttg2 = divmod(g2, NGT)
                    qd = Q(gpsimd, SD3, dma=True)
                    if g2 >= 4:
                        qd.wait(SV, v_stt[2 * (g2 - 4) + 1])
                    qd.dma_start(out=tbufs[g2 % 4][:],
                                 in_=t4_sp[ob2 * 128:(ob2 + 1) * 128,
                                           ttg2 * 4 * GT:(ttg2 + 1) * 4 * GT])
                    d4[g2] = SD3.n
                q = Q(gpsimd, SG, dma=True)
                q.wait(SV, v_stt[2 * grp + 1])
                q.dma_start(
                    out=t5_sp[ob * 128:(ob + 1) * 128,
                              ttg * 4 * GT:(ttg + 1) * 4 * GT],
                    in_=t45w[grp % 3][:])
                g_spill[grp] = SG.n
        return SV.n

    def post_s4(q):
        q.tensor_scalar(out=sc_s[0:1, 21:22], in0=sc_s[0:1, 20:21],
                        scalar1=sc_s[0:1, 22:23], scalar2=None, op0=ALU.mult)
        q.tensor_scalar(out=sc_s[0:1, 23:24], in0=sc_s[0:1, 21:22],
                        scalar1=1.0 / 32767.0, scalar2=1e-12,
                        op0=ALU.mult, op1=ALU.max)
        q.reciprocal(out=sc_s[0:1, 24:25], in_=sc_s[0:1, 23:24])
        q.tensor_scalar(out=sc_s[0:1, 25:26], in0=sc_s[0:1, 24:25],
                        scalar1=sc_s[0:1, 22:23], scalar2=None, op0=ALU.mult)

    L0, L1 = 19, 20
    allreduce(3, gmT4_s, ii_done_v, post_s4)
    bcast_vec(25, 1, 5)

    iii_done_v = stage_iii(gmT5_s)

    def post_s5(q):
        q.tensor_scalar(out=sc_s[0:1, 28:29], in0=sc_s[0:1, 23:24],
                        scalar1=sc_s[0:1, 16:17], scalar2=None, op0=ALU.mult)
        q.tensor_scalar(out=sc_s[0:1, 29:30], in0=sc_s[0:1, 27:28],
                        scalar1=sc_s[0:1, 28:29], scalar2=None, op0=ALU.mult)
        q.tensor_scalar(out=sc_s[0:1, 30:31], in0=sc_s[0:1, 29:30],
                        scalar1=1.0 / 32767.0, scalar2=1e-12,
                        op0=ALU.mult, op1=ALU.max)
        q.reciprocal(out=sc_s[0:1, 31:32], in_=sc_s[0:1, 30:31])
        q.tensor_scalar(out=sc_s[0:1, 26:27], in0=sc_s[0:1, 31:32],
                        scalar1=sc_s[0:1, 28:29], scalar2=None, op0=ALU.mult)
        q.tensor_copy(out=sc_s[0:1, 27:28], in_=sc_s[0:1, 30:31])

    L0, L1 = 26, 27
    allreduce(4, gmT5_s, iii_done_v, post_s5)
    bcast_vec(26, 2, 6)

    @block.sync
    def _(sync):
        q = Q(sync, SY, dma=True)
        q.wait(SC, deq_done_c)
        q.wait(ST)
        q.wait(SV)
        wdv = wdT_s[:].rearrange("p (iob obb o) -> p iob obb o",
                                 iob=NOB, obb=NJ, o=128)
        for obb in range(NJ):
            q.dma_start(
                out=wdv[:, :, obb, :],
                in_=wdgat[obb * 3 * 1024:(obb + 1) * 3 * 1024, :].rearrange(
                    "(c p) o -> p c o", p=128))
    wd_done_y = SY.n

    iv_s0 = SS.n
    iv_v0 = SV.n
    iv_t0 = ST.n
    iv_g0 = SG.n
    NV4 = NOB + NJ
    NT4 = NJ * NOB
    y4 = {}
    s4c = {}
    v4c = {}

    for tt in range(NG):
        @block.sync
        def _(sync, tt=tt):
            q = Q(sync, SY, dma=True)
            if tt == 0:
                q.wait(SV)
                q.wait(SG)
            else:
                q.wait(SV, iv_v0 + NV4 * tt)
            q.dma_start(out=xpr_s[:].rearrange("p (j t) -> p j t", j=NJ),
                        in_=xp_sp[:, tt * GT:(tt + 1) * GT].rearrange(
                            "(j p) t -> p j t", p=128))
            for ob in range(0, 8, 2):
                if tt > 0:
                    q.wait(SS, s4c[(tt - 1, ob + 16)])
                q.dma_start(
                    out=rd_s[ob % 2][:, ((ob // 2) % 4) * GT:
                                     ((ob // 2) % 4 + 1) * GT],
                    in_=t5_sp[ob * 128:(ob + 1) * 128,
                              tt * GT:(tt + 1) * GT])
                y4[(tt, ob)] = (SY, SY.n)

        @block.gpsimd
        def _(gpsimd, tt=tt):
            q = Q(gpsimd, SD2, dma=True)
            if tt == 0:
                q.wait(SV, iv_v0)
                q.wait(SG, iv_g0)
            for ob in range(1, 8, 2):
                if tt > 0:
                    q.wait(SS, s4c[(tt - 1, ob + 16)])
                q.dma_start(
                    out=rd_s[ob % 2][:, ((ob // 2) % 4) * GT:
                                     ((ob // 2) % 4 + 1) * GT],
                    in_=t5_sp[ob * 128:(ob + 1) * 128,
                              tt * GT:(tt + 1) * GT])
                y4[(tt, ob)] = (SD2, SD2.n)

        def emit_load(tt, ob):
            blk_dec = block.sync if ob % 2 == 0 else block.gpsimd
            @blk_dec
            def _(eng, tt=tt, ob=ob):
                q = Q(eng, SY if ob % 2 == 0 else SD2, dma=True)
                q.wait(SS, s4c[(tt, ob - 8)])
                q.dma_start(
                    out=rd_s[ob % 2][:, ((ob // 2) % 4) * GT:
                                     ((ob // 2) % 4 + 1) * GT],
                    in_=t5_sp[ob * 128:(ob + 1) * 128,
                              tt * GT:(tt + 1) * GT])
                y4[(tt, ob)] = (SY, SY.n) if ob % 2 == 0 else (SD2, SD2.n)

        for ob in range(NOB):
            @block.scalar
            def _(scalar, tt=tt, ob=ob):
                q = Q(scalar, SS)
                wI = (iiAi, iiBi)[ob % 2]
                sem, n = y4[(tt, ob)]
                q.wait(sem, n)
                if tt == 0 and ob < 2:
                    q.wait(ST, iv_t0)
                if ob >= 2:
                    q.wait(SV, v4c[(tt, ob - 2)])
                elif tt > 0:
                    q.wait(SV, v4c[(tt - 1, 22 + ob)])
                slot = rd_s[ob % 2][:, ((ob // 2) % 4) * GT:
                                    ((ob // 2) % 4 + 1) * GT]
                q.activation(out=wI[0][:], in_=slot, func=AF.Copy,
                             scale=vec_s[:, 6:7])
                s4c[(tt, ob)] = SS.n

            if ob + 8 < NOB:
                emit_load(tt, ob + 8)

            @block.vector
            def _(vector, tt=tt, ob=ob):
                q = Q(vector, SV)
                wI = (iiAi, iiBi)[ob % 2]
                q.wait(SS, s4c[(tt, ob)])
                half, obl = divmod(ob, 12)
                if ob == 0 and tt > 0:
                    q.wait(ST, iv_t0 + NT4 * tt)
                if ob == 12:
                    q.wait(ST, iv_t0 + NT4 * tt + NT4 // 2)
                q.tensor_copy(out=o16h_s[:, obl * GT:(obl + 1) * GT],
                              in_=wI[0][:])
                v4c[(tt, ob)] = SV.n

        @block.tensor
        def _(tensor, tt=tt):
            q = Q(tensor, ST, serialize=False)
            if tt == 0:
                q.wait(SY, wd_done_y)
            for half in range(2):
                q.wait(SV, v4c[(tt, half * 12 + 11)])
                for ot in range(NJ):
                    if half == 0 and tt > 0:
                        q.wait(SV, iv_v0 + NV4 * (tt - 1) + NOB + ot + 1)
                    for i12 in range(12):
                        iob = half * 12 + i12
                        q.matmul(out=pb[ot][:],
                                 lhsT=wdT_s[:, iob * 1024 + ot * 128:
                                            iob * 1024 + (ot + 1) * 128],
                                 rhs=o16h_s[:, i12 * GT:(i12 + 1) * GT],
                                 start=(iob == 0), stop=(iob == NOB - 1))

        @block.vector
        def _(vector, tt=tt):
            q = Q(vector, SV)
            for ot in range(NJ):
                q.wait(ST, iv_t0 + NT4 * tt + NT4 // 2 + 12 * (ot + 1))
                slot = ost_s[(ot // 4) % 2][:, (ot % 4) * GT:(ot % 4 + 1) * GT]
                if tt > 0 or ot >= 4:
                    ndma = tt * 2 + ot // 4
                    q.wait(SG, iv_g0 + 16 * ndma)
                q.scalar_tensor_tensor(out=slot, in0=pb[ot][:],
                                       scalar=vec_s[:, 7:8],
                                       in1=xpr_s[:, ot * GT:(ot + 1) * GT],
                                       op0=ALU.mult, op1=ALU.add)

        @block.gpsimd
        def _(gpsimd, tt=tt):
            q = Q(gpsimd, SG, dma=True)
            for half in range(2):
                q.wait(SV, iv_v0 + NV4 * tt + NOB + 4 * (half + 1))
                q.dma_start(
                    out=outT[:, tt * GT:(tt + 1) * GT].rearrange(
                        "(j p) t -> p j t", p=128)[:, half * 4:(half + 1) * 4, :],
                    in_=ost_s[half][:].rearrange("p (o t) -> p o t", o=4))

    @block.gpsimd
    def _(gpsimd):
        gpsimd.wait_ge(SG.h, SG.n)

    blk.__exit__(None, None, None)
    return nc



def kernel(L0_attn_hidden_states, L0_attn_Wq, L0_attn_Wf, L0_attn_Wi,
           L0_attn_g_weight, L0_attn_Wo, L0_ffn_w_gate, L0_ffn_w_up,
           L0_ffn_w_down):
    import ml_dtypes
    x = np.asarray(L0_attn_hidden_states, dtype=np.float32)
    base = {
        "wqT": np.ascontiguousarray(np.asarray(L0_attn_Wq, np.float32).T),
        "wfT": np.ascontiguousarray(np.asarray(L0_attn_Wf, np.float32).T),
        "wiT": np.ascontiguousarray(np.asarray(L0_attn_Wi, np.float32).T),
        "woT": np.ascontiguousarray(np.asarray(L0_attn_Wo, np.float32).T),
        "c_gw": np.ascontiguousarray(
            np.asarray(L0_attn_g_weight, np.float32).reshape(NJ, 128).T),
        "c_idn": np.eye(128, dtype=ml_dtypes.bfloat16),
        "c_idnf": np.eye(128, dtype=np.float32),
        "c_mask": np.triu(np.ones((128, 128), ml_dtypes.bfloat16)),
        "c_ones1": np.ones((1, 128), np.float32),
        "c_onesk": np.ones((128, GT), ml_dtypes.bfloat16),
        "c_oneskb": np.ones((128, 1), ml_dtypes.bfloat16),
    }
    nc = build()
    wg = np.asarray(L0_ffn_w_gate, np.float32)
    wu = np.asarray(L0_ffn_w_up, np.float32)
    wd = np.asarray(L0_ffn_w_down, np.float32)
    upc = 2 * NOB // NCORES
    wupc = NJ * 3 // NCORES
    in_maps = []
    for c_i in range(NCORES):
        xc = x[B_CORE * c_i: B_CORE * (c_i + 1)]
        m = dict(base)
        m["xT"] = np.ascontiguousarray(xc.transpose(2, 0, 1).reshape(D, TOK))
        units = []
        for k in range(upc):
            ug = c_i * upc + k
            src = wg if ug < NOB else wu
            ob = ug % NOB
            units.append(src[ob * 128:(ob + 1) * 128, :])
        m["wdq"] = np.ascontiguousarray(np.concatenate(units, axis=0))
        units2 = []
        for k in range(wupc):
            ug = c_i * wupc + k
            obb, c3 = divmod(ug, 3)
            units2.append(wd[obb * 128:(obb + 1) * 128,
                             c3 * 1024:(c3 + 1) * 1024])
        m["wdq2"] = np.ascontiguousarray(np.concatenate(units2, axis=0))
        in_maps.append(m)
    res = run_bass_kernel_spmd(nc, in_maps, core_ids=list(range(NCORES)))
    out = np.empty((B_FULL, T, D), np.float32)
    for c_i in range(NCORES):
        o = res.results[c_i]["outT"].reshape(D, B_CORE, T).transpose(1, 2, 0)
        out[B_CORE * c_i: B_CORE * (c_i + 1)] = o
    return out


# version 26
# speedup vs baseline: 1.0071x; 1.0071x over previous
import math
import numpy as np

import concourse.bass as bass
import concourse.mybir as mybir
from concourse.bass_utils import run_bass_kernel_spmd

F32 = mybir.dt.float32
BF16 = mybir.dt.bfloat16
I32 = mybir.dt.int32
AF = mybir.ActivationFunctionType
ALU = mybir.AluOpType
AX = mybir.AxisListType

NCORES = 8
B_FULL, T, D = 16, 2048, 1024
B_CORE = B_FULL // NCORES
TOK = B_CORE * T
H = 8
GT = 512
NG = TOK // GT
NG_SEQ = T // GT
CH = 128
NCH = GT // CH
DI = 3072
NOB = DI // 128
NJ = D // 128
LNC = -0.5 * math.log(128.0)
LN65536 = math.log(65536.0)
EPS = 1e-5

STAGE = "full"
DBG = None


class Sem:
    def __init__(self, nc, name):
        self.h = nc.alloc_semaphore(name)
        self.n = 0


class Q:

    def __init__(self, eng, sem, serialize=True, dma=False):
        self.e = eng
        self.s = sem
        self.serialize = serialize
        self.step = 16 if dma else 1

    def __getattr__(self, name):
        f = getattr(self.e, name)

        def g(*a, **k):
            if self.serialize and self.s.n > 0:
                self.e.wait_ge(self.s.h, self.s.n)
            ins = f(*a, **k)
            ins.then_inc(self.s.h, self.step)
            self.s.n += self.step
            return ins

        return g

    def wait(self, sem, n=None):
        self.e.wait_ge(sem.h, sem.n if n is None else n)


def _load_act_set(nc, scalar):
    from concourse.hw_specs import get_activation_tables
    for idx, (name, funcs) in enumerate(get_activation_tables(nc.m.arch).items()):
        if AF.Exp in funcs and AF.Ln in funcs:
            return scalar.add_instruction(
                mybir.InstLoadActFuncSet(
                    name=nc.get_next_instruction_name(),
                    act_func_set_id=idx, ins=[], outs=[]))
    raise ValueError("no table set with Exp+Ln")


def build(stage=None, ncores=NCORES):
    stage = stage or STAGE
    nc = bass.Bass("TRN2", target_bir_lowering=False, debug=False)

    xT = nc.dram_tensor("xT", [D, TOK], F32, kind="ExternalInput").ap()
    wqT = nc.dram_tensor("wqT", [D, D], F32, kind="ExternalInput").ap()
    wfT = nc.dram_tensor("wfT", [D, D], F32, kind="ExternalInput").ap()
    wiT = nc.dram_tensor("wiT", [D, D], F32, kind="ExternalInput").ap()
    woT = nc.dram_tensor("woT", [D, D], F32, kind="ExternalInput").ap()
    UPC = 2 * NOB // ncores
    wdq = nc.dram_tensor("wdq", [UPC * 128, D], F32, kind="ExternalInput").ap()
    WUPC = NJ * 3 // ncores
    wdq2 = nc.dram_tensor("wdq2", [WUPC * 128, D], F32,
                          kind="ExternalInput").ap()
    c_gw = nc.dram_tensor("c_gw", [128, NJ], F32, kind="ExternalInput").ap()
    c_idn = nc.dram_tensor("c_idn", [128, 128], BF16, kind="ExternalInput").ap()
    c_idnf = nc.dram_tensor("c_idnf", [128, 128], F32, kind="ExternalInput").ap()
    c_mask = nc.dram_tensor("c_mask", [128, 128], BF16, kind="ExternalInput").ap()
    c_ones1 = nc.dram_tensor("c_ones1", [1, 128], F32, kind="ExternalInput").ap()
    c_onesk = nc.dram_tensor("c_onesk", [128, GT], BF16, kind="ExternalInput").ap()
    c_oneskb = nc.dram_tensor("c_oneskb", [128, 1], BF16, kind="ExternalInput").ap()

    outT = nc.dram_tensor("outT", [D, TOK], F32, kind="ExternalOutput").ap()
    xp_sp = nc.dram_tensor("xp_sp", [D, TOK], F32, kind="Internal").ap()

    full = stage == "full"
    if full:
        g_sp = nc.dram_tensor("g_sp", [DI, TOK], F32, kind="Internal").ap()
        u_sp = nc.dram_tensor("u_sp", [DI, TOK], F32, kind="Internal").ap()
        t4_sp = nc.dram_tensor("t4_sp", [DI, TOK], F32, kind="Internal").ap()
        t5_sp = nc.dram_tensor("t5_sp", [DI, TOK], F32, kind="Internal").ap()
        _sh = "Shared" if ncores > 4 else "Local"
        wloc = nc.dram_tensor("wloc", [UPC * 1024, 128], BF16,
                              kind="Internal").ap()
        wgat = nc.dram_tensor("wgat", [2 * NOB * 1024, 128], BF16,
                              kind="Internal", addr_space=_sh).ap()
        wdloc = nc.dram_tensor("wdloc", [WUPC * 1024, 128], BF16,
                               kind="Internal").ap()
        wdgat = nc.dram_tensor("wdgat", [NJ * 3 * 1024, 128], BF16,
                               kind="Internal", addr_space=_sh).ap()
        cc_in = [nc.dram_tensor(f"cc_in{i}", [1, 1], F32, kind="Internal").ap()
                 for i in range(5)]
        cc_out = [nc.dram_tensor(f"cc_out{i}", [1, 1], F32, kind="Internal",
                                 addr_space="Shared" if ncores > 4 else "Local"
                                 ).ap() for i in range(5)]

    PSIZE = nc.SBUF_PARTITION_SIZE_BYTES
    cur = [((PSIZE - nc.sbuf_bytes_remaining) + 31) // 32 * 32]

    sb_at = {}

    def sba(name, shape, dtype, at=None):
        esz = 2 if dtype in (BF16, mybir.dt.float16) else 4
        nbytes = int(np.prod(shape[1:])) * esz
        if at is None:
            at = cur[0]
            cur[0] = at + ((nbytes + 31) // 32) * 32
        assert at + nbytes <= PSIZE - 32, f"sbuf overflow {name}: {at + nbytes}"
        sb_at[name] = at
        return nc.alloc_sbuf_tensor_at(name, shape, dtype, offset=at)

    R0 = cur[0]
    wq_s = sba("wq_s", [128, NJ * D], BF16)
    wf_s = sba("wf_s", [128, NJ * D], BF16)
    wi_s = sba("wi_s", [128, NJ * D], BF16)
    wo_s = sba("wo_s", [128, NJ * D], BF16)
    R0_END = cur[0]

    R1 = cur[0]
    xg_s = [sba(f"xg{i}", [128, NJ * GT], F32) for i in range(2)]
    xn_s = sba("xn_s", [128, NJ * GT], BF16)
    of_s = sba("of_s", [128, NJ * GT], F32)
    ofb_s = sba("ofb_s", [128, NJ * GT], BF16)
    st_f = sba("st_f", [128, H * 128], F32)
    st_b = sba("st_b", [128, H * 128], BF16)
    lq_s = sba("lq_s", [128, GT], F32)
    lf_s = sba("lf_s", [128, GT], F32)
    lt_s = sba("lt_s", [128, GT], F32)
    uqt_s = sba("uqt_s", [128, GT], F32)
    tk_s = sba("tk_s", [128, GT], F32)
    vb_s = sba("vb_s", [128, GT], BF16)
    eqa_s = sba("eqa_s", [128, 128], F32)
    eqb_s = sba("eqb_s", [128, 64], F32)
    qta_s = sba("qta_s", [128, 128], BF16)
    qtb_s = sba("qtb_s", [128, 64], BF16)
    kta_s = sba("kta_s", [128, 64], BF16)
    ktb_s = sba("ktb_s", [128, 128], BF16)
    khat_s = sba("khat_s", [128, 128], BF16)
    vT_s = sba("vT_s", [128, 128], BF16)
    khT_s = sba("khT_s", [128, 128], BF16)
    pt_s = sba("pt_s", [128, 128], BF16)
    bias_s = sba("bias_s", [128, 32], F32)
    qp_s = sba("qp_s", [128, GT], F32)
    R1_END = cur[0]

    gw_s = sba("gw_s", [128, NJ], F32)
    mask_s = sba("mask_s", [128, 128], BF16)
    oneskb_s = sba("oneskb_s", [128, 1], BF16)
    onesk_s = sba("onesk_s", [128, GT], BF16)
    pairx = [sba(f"pairx{i}", [128, GT], BF16, at=sb_at["ofb_s"] + i * GT * 2)
             for i in range(2)]
    pairq = [sba(f"pairq{i}", [128, GT], BF16,
                 at=sb_at["ofb_s"] + (2 + i) * GT * 2) for i in range(2)]
    r1bc_s = sba("r1bc_s", [128, GT], F32)
    mr1bc_s = sba("mr1bc_s", [128, GT], F32)
    tmp1_s = sba("tmp1_s", [128, GT], F32, at=sb_at["of_s"])
    srow_s = sba("srow_s", [1, GT], F32)
    vrow_s = sba("vrow_s", [1, GT], F32)
    mrow_s = sba("mrow_s", [1, GT], F32)
    R1X_END = cur[0]
    idn_s = sba("idn_s", [128, 128], BF16)
    ones1_s = sba("ones1_s", [1, 128], F32)
    r2row_s = sba("r2row_s", [1, TOK], F32)
    mr2row_s = sba("mr2row_s", [1, TOK], F32)
    x2max_s = sba("x2max_s", [128, 1], F32)
    gmU_s = sba("gmU_s", [128, 1], F32)
    gmG_s = sba("gmG_s", [128, 1], F32)
    gmT4_s = sba("gmT4_s", [128, 1], F32)
    gmT5_s = sba("gmT5_s", [128, 1], F32)
    redt_s = sba("redt_s", [128, 1], F32)
    sc_s = sba("sc_s", [1, 32], F32)
    vec_s = sba("vec_s", [128, 16], F32)
    idnf_s = sba("idnf_s", [128, 128], F32)
    wraw_s = [sba("wraw0", [128, D], F32)]
    dqt_s = sba("dqt_s", [128, D], F32)
    dqi_s = sba("dqi_s", [128, D], I32, at=sb_at["dqt_s"])
    dqb_s = sba("dqb_s", [128, D], BF16)
    dqm_s = sba("dqm_s", [128, 32], F32)
    dqr_s = sba("dqr_s", [128, 32], F32)
    dqo_s = sba("dqo_s", [128, NJ * 128], BF16)
    R2_END = cur[0]

    if full:
        x1b_s = sba("x1b_s", [128, NJ * TOK], BF16, at=R0)
        wdT_s = sba("wdT_s", [128, NOB * 1024], BF16, at=R0)
        t45g_s = [sba(f"t45g{i}", [128, 4 * GT], F32,
                      at=R0 + NOB * 1024 * 2 + i * 8192) for i in range(2)]
        ost_s = [sba(f"ost{i}", [128, 4 * GT], F32,
                     at=R0 + NOB * 1024 * 2 + i * 8192) for i in range(2)]
        t45a_s = [sba(f"t45a{i}", [128, 4 * GT], F32, at=R1 + i * 8192)
                  for i in range(2)]
        xpr_s = sba("xpr_s", [128, NJ * GT], F32, at=R1)
        rd_s = [sba(f"rd{i}", [128, 4 * GT], F32, at=R1 + 16384 + i * 8192)
                for i in range(4)]
        wt_s = [sba(f"wt{i}", [128, NJ * 128], BF16, at=R1 + 49152 + i * 2048)
                for i in range(2)]
        stg_s = [sba(f"stg{i}", [128, 4 * GT], F32, at=R1 + 53248 + i * 8192)
                 for i in range(2)]
        iiA = (sba("iiA0", [128, GT], F32, at=sb_at["wraw0"]),
               sba("iiA1", [128, GT], F32, at=sb_at["wraw0"] + 2048),
               sba("iiA2", [128, GT], F32, at=sb_at["dqt_s"]))
        iiB = (sba("iiB0", [128, GT], F32, at=sb_at["dqt_s"] + 2048),
               sba("iiB1", [128, GT], F32, at=sb_at["dqb_s"]),
               sba("iiB2", [128, GT], F32, at=sb_at["dqo_s"]))
        iiAi = (sba("iiAi0", [128, GT], I32, at=sb_at["iiA0"]),
                sba("iiAi1", [128, GT], I32, at=sb_at["iiA1"]),
                sba("iiAi2", [128, GT], I32, at=sb_at["iiA2"]))
        iiBi = (sba("iiBi0", [128, GT], I32, at=sb_at["iiB0"]),
                sba("iiBi1", [128, GT], I32, at=sb_at["iiB1"]),
                sba("iiBi2", [128, GT], I32, at=sb_at["iiB2"]))
        o16h_s = sba("o16h", [128, 12 * GT], BF16, at=R1 + 16384 + 16384)
        xf1 = sba("xf1", [128, GT], F32, at=sb_at["qp_s"])
        xfi = sba("xfi", [128, GT], I32, at=sb_at["qp_s"])
        xf2 = (sba("xf2a", [128, GT], F32, at=sb_at["dqb_s"]),
               sba("xf2b", [128, GT], F32, at=sb_at["dqo_s"]))
        gii = (sba("gii0", [128, 1024], I32, at=sb_at["wraw0"]),
               sba("gii1", [128, 1024], I32, at=sb_at["dqt_s"]))
        assert R1 + 69632 + 8192 <= R1_END, (R1, R1_END)
        sfb = [sba(f"sfb{i}", [128, 1024], F32, at=R1 + 69632 + i * 4096)
               for i in range(2)]
        sib = [sba(f"sib{i}", [128, 1024], I32, at=R1 + 69632 + i * 4096)
               for i in range(2)]

    pb = [nc.alloc_psum_tensor(f"pb{i}", [128, GT], F32) for i in range(8)]
    vT_ps = pb[4][:, 0:64].bitcast(BF16)
    khT_ps = pb[4][:, 64:128].bitcast(BF16)
    pt_ps = pb[7][:, 0:128]
    ds_ps = pb[7][:, 128:256]

    SV = Sem(nc, "SV")
    SS = Sem(nc, "SS")
    ST = Sem(nc, "ST")
    SG = Sem(nc, "SG")
    SY = Sem(nc, "SY")
    SC = Sem(nc, "SC")
    SB = Sem(nc, "SB")
    SD = Sem(nc, "SD")
    SD2 = Sem(nc, "SD2")
    SD3 = Sem(nc, "SD3")

    PH0_NLOAD = 8

    blk = nc.Block()
    block = blk.__enter__()

    @block.sync
    def _(sync):
        q = Q(sync, SY, dma=True)
        q.dma_start(out=idn_s[:], in_=c_idn)
        q.dma_start(out=idnf_s[:], in_=c_idnf)
        q.dma_start(out=mask_s[:], in_=c_mask)
        q.dma_start(out=ones1_s[:], in_=c_ones1)
        q.dma_start(out=onesk_s[:], in_=c_onesk)
        q.dma_start(out=oneskb_s[:], in_=c_oneskb)
        q.dma_start(out=gw_s[:], in_=c_gw)
        q.dma_start(out=xg_s[0][:].rearrange("p (j t) -> p j t", j=NJ),
                    in_=xT[:, 0:GT].rearrange("(j p) t -> p j t", p=128))
    ph0_sy = SY.n

    @block.gpsimd
    def _(gpsimd):
        q = Q(gpsimd, SG, dma=True)
        for w_hbm, w_sb in ((wqT, wq_s), (wfT, wf_s), (wiT, wi_s), (woT, wo_s)):
            q.dma_start(out=w_sb[:].rearrange("p (j o) -> p j o", j=NJ),
                        in_=w_hbm.rearrange("(j p) o -> p j o", p=128))
    ph0_g = SG.n

    @block.scalar
    def _(scalar):
        _load_act_set(nc, scalar)

    @block.vector
    def _(vector):
        q = Q(vector, SV)
        q.memset(st_f[:], 0.0)
        q.memset(st_b[:], 0.0)
        q.memset(x2max_s[:], 0.0)
        q.memset(gmU_s[:], 0.0)
        q.memset(gmG_s[:], 0.0)
        q.memset(gmT4_s[:], 0.0)
        q.memset(gmT5_s[:], 0.0)
        q.memset(vec_s[:, 3:4], -LN65536)
        q.memset(vec_s[:, 9:10], 1.0 / 65536.0)
        for i in range(8):
            q.memset(pb[i][:], 0.0)

    f1_ss = 0
    deq_done_c = 0
    if full:
        deq_units = ([(wdq, wloc, u) for u in range(UPC)]
                     + [(wdq2, wdloc, k) for k in range(WUPC)])
        for ui, (dq_src, dq_dst, u) in enumerate(deq_units):
            @block.sync
            def _(sync, ui=ui, dq_src=dq_src, u=u):
                q = Q(sync, SY, dma=True)
                if ui > 0:
                    q.wait(SV)
                q.dma_start(out=wraw_s[0][:],
                            in_=dq_src[u * 128:(u + 1) * 128, :])

            @block.vector
            def _(vector, ui=ui):
                q = Q(vector, SV)
                q.wait(SY)
                w3 = wraw_s[0][:].rearrange("p (b i) -> p b i", b=32)
                q.reduce_max(out=dqm_s[:], in_=w3, axis=AX.X,
                             apply_absolute_value=True)
                q.tensor_scalar(out=dqm_s[:], in0=dqm_s[:], scalar1=1.0 / 7.0,
                                scalar2=1e-12, op0=ALU.mult, op1=ALU.max)
                q.reciprocal(out=dqr_s[:], in_=dqm_s[:])
                rb = dqr_s[:].rearrange("p (b one) -> p b one",
                                        one=1).broadcast_to([128, 32, 32])
                q.tensor_tensor(out=dqt_s[:].rearrange("p (b i) -> p b i",
                                                       b=32),
                                in0=w3, in1=rb, op=ALU.mult)

            @block.vector
            def _(vector, ui=ui):
                q = Q(vector, SV)
                q.wait(ST)
                q.tensor_copy(out=dqi_s[:], in_=dqt_s[:])
                sb3 = dqm_s[:].rearrange("p (b one) -> p b one",
                                         one=1).broadcast_to([128, 32, 32])
                q.scalar_tensor_tensor(
                    out=dqb_s[:].rearrange("p (b i) -> p b i", b=32),
                    in0=dqi_s[:].rearrange("p (b i) -> p b i", b=32),
                    scalar=0.0, in1=sb3, op0=ALU.add, op1=ALU.mult)

            tq0 = ST.n
            sq0 = SS.n
            gq0 = SG.n

            @block.tensor
            def _(tensor, sq0=sq0, ui=ui, gq0=gq0):
                q = Q(tensor, ST, serialize=False)
                q.wait(SV)
                if sq0:
                    q.wait(SS, sq0)
                if ui > 0:
                    q.wait(SG, gq0)
                for j in range(NJ):
                    if j >= 2:
                        q.wait(SS, sq0 + (j - 1))
                    quarter = pb[6 + j % 2][:, 0:64].bitcast(BF16)
                    q.transpose(out=quarter,
                                in_=dqb_s[:, j * 128:(j + 1) * 128],
                                identity=idn_s[:])

            @block.scalar
            def _(scalar, tq0=tq0):
                q = Q(scalar, SS)
                for j in range(NJ):
                    q.wait(ST, tq0 + j + 1)
                    quarter = pb[6 + j % 2][:, 0:64].bitcast(BF16)
                    q.copy(out=dqo_s[:, j * 128:(j + 1) * 128], in_=quarter)

            @block.gpsimd
            def _(gpsimd, dq_dst=dq_dst, u=u):
                q = Q(gpsimd, SG, dma=True)
                q.wait(SS)
                q.dma_start(
                    out=dq_dst[u * 1024:(u + 1) * 1024, :].rearrange(
                        "(j p) o -> p j o", p=128),
                    in_=dqo_s[:].rearrange("p (j o) -> p j o", j=NJ))

        @block.gpsimd
        def _(gpsimd):
            gpsimd.wait_ge(SG.h, SG.n)
            for ins_t, outs_t in ((wloc, wgat), (wdloc, wdgat)):
                ins = gpsimd.collective_compute(
                    "AllGather", ALU.bypass,
                    replica_groups=[list(range(ncores))],
                    ins=[ins_t], outs=[outs_t])
                ins.then_inc(SC.h, 1)
                SC.n += 1
        deq_done_c = SC.n
        f1_ss = SS.n

    spill_sg = {}
    xload_sy = {0: ph0_sy}
    s1_carry = [0]
    v1_carry = [0]
    t2_carry = [0]
    s2_carry = [0]
    for g in range(NG):
        xg = xg_s[g % 2]
        gbase = g * GT

        t_pre = ST.n
        v_pre = SV.n

        @block.vector
        def _(vector, g=g, xg=xg, t_pre=t_pre):
            q = Q(vector, SV)
            if g > 0:
                q.wait(ST)
                if DBG:
                    q.wait(SG)
            if g % NG_SEQ == 0 and g > 0:
                q.memset(st_f[:], 0.0)
                q.memset(st_b[:], 0.0)
            q.wait(SY, xload_sy[g])
            if g == 0:
                q.wait(SG, ph0_g)
            for j in range(NJ):
                if j >= 2:
                    q.wait(ST, t_pre + 2 * (j - 1))
                xj = xg[:, j * GT:(j + 1) * GT]
                q.tensor_copy(out=pairx[j % 2][:], in_=xj)
                q.tensor_tensor(out=pairq[j % 2][:], in0=pairx[j % 2][:],
                                in1=pairx[j % 2][:], op=ALU.mult)

        @block.tensor
        def _(tensor, g=g, v_pre=v_pre):
            q = Q(tensor, ST, serialize=False)
            if g == 0 and f1_ss:
                q.wait(SS, f1_ss)
            for j in range(NJ):
                vbump = 2 if g % NG_SEQ == 0 and g > 0 else 0
                q.wait(SV, v_pre + vbump + 2 * (j + 1))
                q.matmul(out=pb[5][0:1, :], lhsT=oneskb_s[:], rhs=pairx[j % 2][:],
                         start=(j == 0), stop=(j == NJ - 1))
                q.matmul(out=pb[6][0:1, :], lhsT=oneskb_s[:], rhs=pairq[j % 2][:],
                         start=(j == 0), stop=(j == NJ - 1))

        @block.vector
        def _(vector, g=g):
            q = Q(vector, SV)
            q.wait(ST)
            q.tensor_scalar(out=mrow_s[:], in0=pb[5][0:1, :], scalar1=1.0 / D,
                            scalar2=None, op0=ALU.mult)
            q.tensor_scalar(out=vrow_s[:], in0=pb[6][0:1, :], scalar1=1.0 / D,
                            scalar2=EPS, op0=ALU.mult, op1=ALU.add)
            q.tensor_tensor(out=srow_s[:], in0=mrow_s[:], in1=mrow_s[:],
                            op=ALU.mult)
            q.tensor_tensor(out=vrow_s[:], in0=vrow_s[:], in1=srow_s[:],
                            op=ALU.subtract)

        @block.scalar
        def _(scalar, g=g):
            q = Q(scalar, SS)
            q.wait(SV)
            q.activation(out=vrow_s[:], in_=vrow_s[:], func=AF.Ln)
            q.activation(out=srow_s[:], in_=vrow_s[:], func=AF.Exp, scale=-0.5)

        @block.vector
        def _(vector, g=g):
            q = Q(vector, SV)
            q.wait(SS)
            q.tensor_tensor(out=mrow_s[:], in0=mrow_s[:], in1=srow_s[:],
                            op=ALU.mult)

        @block.tensor
        def _(tensor, g=g):
            q = Q(tensor, ST, serialize=False)
            q.wait(SV)
            q.matmul(out=pb[5][:], lhsT=ones1_s[:], rhs=srow_s[:],
                     start=True, stop=True)
            q.matmul(out=pb[6][:], lhsT=ones1_s[:], rhs=mrow_s[:],
                     start=True, stop=True)

        @block.vector
        def _(vector, g=g, xg=xg):
            q = Q(vector, SV)
            q.wait(ST)
            q.tensor_copy(out=r1bc_s[:], in_=pb[5][:])
            q.tensor_copy(out=mr1bc_s[:], in_=pb[6][:])
            for j in range(NJ):
                xj = xg[:, j * GT:(j + 1) * GT]
                q.tensor_tensor(out=tmp1_s[:], in0=xj, in1=r1bc_s[:],
                                op=ALU.mult)
                q.tensor_tensor(out=xn_s[:, j * GT:(j + 1) * GT], in0=tmp1_s[:],
                                in1=mr1bc_s[:], op=ALU.subtract)
        xn_done = SV.n

        if DBG == "xg":
            @block.gpsimd
            def _(gpsimd, g=g, gbase=gbase, xg=xg):
                q = Q(gpsimd, SG, dma=True)
                q.wait(SY, 16 * 7 + 16 * g)
                q.dma_start(out=outT[:, gbase:gbase + GT].rearrange(
                    "(j p) t -> p j t", p=128),
                    in_=xg[:].rearrange("p (j t) -> p j t", j=NJ))

        if DBG == "sums":
            @block.gpsimd
            def _(gpsimd, g=g, gbase=gbase):
                q = Q(gpsimd, SG, dma=True)
                q.wait(SV)
                q.dma_start(out=outT[0:1, gbase:gbase + GT], in_=mrow_s[:])
                q.dma_start(out=outT[1:2, gbase:gbase + GT], in_=vrow_s[:])
                q.dma_start(out=outT[2:3, gbase:gbase + GT], in_=srow_s[:])

        if DBG == "xn":
            @block.gpsimd
            def _(gpsimd, g=g, gbase=gbase):
                q = Q(gpsimd, SG, dma=True)
                q.wait(SV, xn_done)
                q.dma_start(out=outT[:, gbase:gbase + GT].rearrange(
                    "(j p) t -> p j t", p=128),
                    in_=xn_s[:].rearrange("p (j t) -> p j t", j=NJ))

        st_ready = SV.n

        def emit_t1(h, v1_prev, s1_prev):
            @block.tensor
            def _(tensor, h=h, xn_done=xn_done, v1_prev=v1_prev,
                  s1_prev=s1_prev):
                q = Q(tensor, ST, serialize=False)
                q.wait(SV, xn_done)
                if v1_prev:
                    q.wait(SV, v1_prev)
                if s1_prev:
                    q.wait(SS, s1_prev)
                for (wsb, bank) in ((wq_s, 0), (wf_s, 1), (wi_s, 2)):
                    for j in range(NJ):
                        q.matmul(
                            out=pb[bank][:],
                            lhsT=wsb[:, j * D + h * 128: j * D + (h + 1) * 128],
                            rhs=xn_s[:, j * GT:(j + 1) * GT],
                            start=(j == 0), stop=(j == NJ - 1))
            return ST.n

        t1_end = emit_t1(0, 0, 0)
        for h in range(H):
            @block.vector
            def _(vector, g=g, h=h, t1_end=t1_end):
                q = Q(vector, SV)
                if DBG:
                    q.wait(SG)
                q.wait(ST, t1_end)
                if s2_carry[0]:
                    q.wait(SS, s2_carry[0])
                if t2_carry[0]:
                    q.wait(ST, t2_carry[0])
                q.tensor_copy(out=qp_s[:], in_=pb[0][:])
                q.tensor_copy(out=uqt_s[:], in_=pb[1][:])
                q.tensor_copy(out=vb_s[:], in_=pb[2][:])
            v1_mid = SV.n

            @block.scalar
            def _(scalar, g=g, h=h, v1_mid=v1_mid, v1_prev=v1_carry[0]):
                q = Q(scalar, SS)
                q.wait(SV, v1_mid)
                if v1_prev:
                    q.wait(SV, v1_prev)
                q.activation(out=lq_s[:], in_=qp_s[:], func=AF.Exp, scale=-1.0)
                q.activation(out=lq_s[:], in_=lq_s[:], func=AF.Ln, bias=1.0)
                q.activation(out=lf_s[:], in_=uqt_s[:], func=AF.Exp,
                             scale=-1.0)
                q.activation(out=lf_s[:], in_=lf_s[:], func=AF.Ln, bias=1.0)
            s1_end = SS.n
            s1_carry[0] = s1_end

            @block.vector
            def _(vector, g=g, h=h, s1_end=s1_end):
                q = Q(vector, SV)
                q.wait(SS, s1_end)
                q.tensor_tensor_scan(out=lt_s[:], data0=onesk_s[:],
                                     data1=lf_s[:], initial=0.0,
                                     op0=ALU.mult, op1=ALU.add)
                q.tensor_scalar(out=tk_s[:, 0:1], in0=uqt_s[:, 0:1],
                                scalar1=-1.0, scalar2=None, op0=ALU.mult)
                q.tensor_tensor(out=tk_s[:, 1:GT], in0=lt_s[:, 0:GT - 1],
                                in1=uqt_s[:, 1:GT], op=ALU.subtract)
                q.tensor_tensor(out=uqt_s[:], in0=lq_s[:], in1=lt_s[:],
                                op=ALU.add)
                lt127 = lt_s[:, 127::128]
                lt63 = lt_s[:, 63::128]
                q.memset(bias_s[:, 0:1], LNC)
                q.tensor_scalar(out=bias_s[:, 1:4], in0=lt127[:, 0:3],
                                scalar1=1.0, scalar2=LNC, op0=ALU.mult,
                                op1=ALU.add)
                q.tensor_scalar(out=bias_s[:, 4:8], in0=lt63, scalar1=1.0,
                                scalar2=LNC, op0=ALU.mult, op1=ALU.add)
                q.memset(bias_s[:, 8:9], 0.0)
                q.tensor_scalar(out=bias_s[:, 9:12], in0=lt127[:, 0:3],
                                scalar1=-1.0, scalar2=None, op0=ALU.mult)
                q.tensor_scalar(out=bias_s[:, 12:16], in0=lt63, scalar1=-1.0,
                                scalar2=None, op0=ALU.mult)
                q.tensor_scalar(out=bias_s[:, 16:20], in0=lt127, scalar1=-1.0,
                                scalar2=None, op0=ALU.mult)
                q.tensor_copy(out=bias_s[:, 20:21], in_=bias_s[:, 16:17])
                q.tensor_tensor(out=bias_s[:, 21:24], in0=lt127[:, 0:3],
                                in1=lt127[:, 1:4], op=ALU.subtract)
            v1_end = SV.n

            if DBG in ("lf", "lt", "tk", "uqt", "lq") and h == 0:
                buf = {"lf": lf_s, "lt": lt_s, "tk": tk_s, "uqt": uqt_s,
                       "lq": lq_s}[DBG]

                @block.gpsimd
                def _(gpsimd, g=g, buf=buf, v1_end=v1_end):
                    q = Q(gpsimd, SG, dma=True)
                    q.wait(SV, v1_end)
                    q.dma_start(out=outT[0:128, g * GT:(g + 1) * GT],
                                in_=buf[:])

            v1_carry[0] = v1_end
            t1_next = emit_t1(h + 1, v1_mid, 0) if h + 1 < H else None
            s2_pre = SS.n
            t2_pre = ST.n
            v2_pre = SV.n
            NS2 = 8
            NT2 = 8
            NV2 = 5

            @block.scalar
            def _(scalar, g=g, h=h, v1_end=v1_end, v2_pre=v2_pre,
                  t2_pre=t2_pre, t2_prev=t2_carry[0]):
                q = Q(scalar, SS)
                for c in range(NCH):
                    cs = c * CH
                    q.wait(SV, v1_end)
                    if c == 0 and t2_prev:
                        q.wait(ST, t2_prev)
                    if c > 0:
                        q.wait(SV, v2_pre + NV2 * (c - 1) + 2)
                        q.wait(ST, t2_pre + NT2 * (c - 1) + 5)
                    q.activation(out=eqa_s[:], in_=uqt_s[:, cs:cs + CH],
                                 func=AF.Exp, scale=-1.0,
                                 bias=bias_s[:, c:c + 1])
                    q.activation(out=eqb_s[:], in_=uqt_s[:, cs + 64:cs + CH],
                                 func=AF.Exp, scale=-1.0,
                                 bias=bias_s[:, 4 + c:5 + c])
                    q.activation(out=kta_s[:], in_=tk_s[:, cs:cs + 64],
                                 func=AF.Exp, bias=bias_s[:, 8 + c:9 + c])
                    q.activation(out=ktb_s[:], in_=tk_s[:, cs:cs + CH],
                                 func=AF.Exp, bias=bias_s[:, 12 + c:13 + c])
                    q.activation(out=khat_s[:], in_=tk_s[:, cs:cs + CH],
                                 func=AF.Exp, bias=bias_s[:, 16 + c:17 + c])
                    q.activation(out=bias_s[:, 24 + c:25 + c],
                                 in_=bias_s[:, 20 + c:21 + c], func=AF.Exp)
                    q.wait(ST, t2_pre + NT2 * c + 2)
                    q.copy(out=vT_s[:], in_=vT_ps)
                    q.copy(out=khT_s[:], in_=khT_ps)
            s2_end = SS.n
            s2_carry[0] = s2_end

            @block.tensor
            def _(tensor, g=g, h=h, v1_end=v1_end, s2_pre=s2_pre,
                  v2_pre=v2_pre, st_ready=st_ready):
                q = Q(tensor, ST, serialize=False)
                for c in range(NCH):
                    cs = c * CH
                    q.wait(SV, v1_end)
                    q.wait(SS, s2_pre + NS2 * c + 5)
                    q.transpose(out=vT_ps, in_=vb_s[:, cs:cs + CH],
                                identity=idn_s[:])
                    q.transpose(out=khT_ps, in_=khat_s[:], identity=idn_s[:])
                    q.wait(SV, v2_pre + NV2 * c + 2)
                    q.matmul(out=pt_ps[0:64, 0:64], lhsT=kta_s[:],
                             rhs=qta_s[:, 0:64], start=True, stop=True)
                    q.matmul(out=pt_ps[0:64, 64:128], lhsT=ktb_s[:, 0:64],
                             rhs=qtb_s[:], start=True, stop=True)
                    q.matmul(out=pt_ps[64:128, 64:128], lhsT=ktb_s[:, 64:128],
                             rhs=qtb_s[:], start=True, stop=True)
                    q.wait(SV, v2_pre + NV2 * c + 3)
                    if c == 0:
                        q.wait(SV, st_ready)
                    q.wait(SS, s2_pre + NS2 * c + 8)
                    otq = pb[3][:, cs:cs + CH]
                    q.matmul(out=otq, lhsT=vT_s[:], rhs=pt_s[:],
                             start=True, stop=False)
                    q.matmul(out=otq, lhsT=st_b[:, h * 128:(h + 1) * 128],
                             rhs=qta_s[:], start=False, stop=True)
                    q.matmul(out=ds_ps, lhsT=khT_s[:], rhs=vT_s[:],
                             start=True, stop=True)
            t2_end = ST.n
            t2_carry[0] = t2_end

            @block.vector
            def _(vector, g=g, h=h, s2_pre=s2_pre, t2_pre=t2_pre):
                q = Q(vector, SV)
                for c in range(NCH):
                    cs = c * CH
                    q.wait(SS, s2_pre + NS2 * c + 6)
                    q.tensor_tensor(out=qta_s[:], in0=qp_s[:, cs:cs + CH],
                                    in1=eqa_s[:], op=ALU.mult)
                    q.tensor_tensor(out=qtb_s[:], in0=qp_s[:, cs + 64:cs + CH],
                                    in1=eqb_s[:], op=ALU.mult)
                    q.wait(ST, t2_pre + NT2 * c + 5)
                    q.scalar_tensor_tensor(out=pt_s[:], in0=pt_ps, scalar=1.0,
                                           in1=mask_s[:], op0=ALU.mult,
                                           op1=ALU.mult)
                    q.wait(ST, t2_pre + NT2 * c + 8)
                    sl = st_f[:, h * 128:(h + 1) * 128]
                    q.scalar_tensor_tensor(out=sl, in0=sl,
                                           scalar=bias_s[:, 24 + c:25 + c],
                                           in1=ds_ps, op0=ALU.mult, op1=ALU.add)
                    q.tensor_copy(out=st_b[:, h * 128:(h + 1) * 128], in_=sl)
            st_ready = SV.n

            @block.scalar
            def _(scalar, g=g, h=h, t2_end=t2_end):
                q = Q(scalar, SS)
                q.wait(ST, t2_end)
                q.copy(out=of_s[:, h * GT:(h + 1) * GT], in_=pb[3][:])
            of_s_end = SS.n
            t1_end = t1_next

        if DBG == "of":
            @block.gpsimd
            def _(gpsimd, g=g, gbase=gbase):
                q = Q(gpsimd, SG, dma=True)
                q.wait(SV)
                q.dma_start(out=outT[:, gbase:gbase + GT].rearrange(
                    "(j p) t -> p j t", p=128),
                    in_=of_s[:].rearrange("p (j t) -> p j t", j=NJ))

        tr_pre = ST.n
        vr_pre = SV.n

        @block.vector
        def _(vector, g=g, tr_pre=tr_pre, of_s_end=of_s_end):
            q = Q(vector, SV)
            q.wait(SS, of_s_end)
            for j in range(NJ):
                if j >= 2:
                    q.wait(ST, tr_pre + (j - 1))
                ofj = of_s[:, j * GT:(j + 1) * GT]
                q.tensor_tensor(out=pairx[j % 2][:], in0=ofj, in1=ofj,
                                op=ALU.mult)

        @block.tensor
        def _(tensor, g=g, vr_pre=vr_pre):
            q = Q(tensor, ST, serialize=False)
            for j in range(NJ):
                q.wait(SV, vr_pre + j + 1)
                q.matmul(out=pb[1][0:1, :], lhsT=oneskb_s[:], rhs=pairx[j % 2][:],
                         start=(j == 0), stop=(j == NJ - 1))

        @block.vector
        def _(vector, g=g):
            q = Q(vector, SV)
            q.wait(ST)
            q.tensor_scalar(out=srow_s[:], in0=pb[1][0:1, :], scalar1=1.0 / D,
                            scalar2=EPS, op0=ALU.mult, op1=ALU.add)

        @block.scalar
        def _(scalar, g=g):
            q = Q(scalar, SS)
            q.wait(SV)
            q.activation(out=srow_s[:], in_=srow_s[:], func=AF.Ln)
            q.activation(out=srow_s[:], in_=srow_s[:], func=AF.Exp, scale=-0.5)

        @block.tensor
        def _(tensor, g=g):
            q = Q(tensor, ST, serialize=False)
            q.wait(SS)
            q.matmul(out=pb[2][:], lhsT=ones1_s[:], rhs=srow_s[:],
                     start=True, stop=True)

        @block.vector
        def _(vector, g=g):
            q = Q(vector, SV)
            q.wait(ST)
            q.tensor_copy(out=r1bc_s[:], in_=pb[2][:])
            for j in range(NJ):
                q.tensor_tensor(out=of_s[:, j * GT:(j + 1) * GT],
                                in0=of_s[:, j * GT:(j + 1) * GT],
                                in1=r1bc_s[:], op=ALU.mult)
        vofn_end = SV.n

        @block.scalar
        def _(scalar, g=g, vofn_end=vofn_end):
            q = Q(scalar, SS)
            for j in range(NJ):
                q.wait(SV, vofn_end - NJ + j + 1)
                q.activation(out=ofb_s[:, j * GT:(j + 1) * GT],
                             in_=of_s[:, j * GT:(j + 1) * GT],
                             func=AF.Copy, scale=gw_s[:, j:j + 1])
        sofb_end = SS.n

        vres_pre = SV.n

        @block.tensor
        def _(tensor, g=g, sofb_end=sofb_end, vres_pre=vres_pre):
            q = Q(tensor, ST, serialize=False)
            q.wait(SS, sofb_end)
            for ot in range(NJ):
                if ot >= 2:
                    q.wait(SV, vres_pre + (ot - 1))
                bank = pb[5] if ot % 2 == 0 else pb[6]
                for j in range(NJ):
                    q.matmul(
                        out=bank[:],
                        lhsT=wo_s[:, j * D + ot * 128: j * D + (ot + 1) * 128],
                        rhs=ofb_s[:, j * GT:(j + 1) * GT],
                        start=(j == 0), stop=(j == NJ - 1))
        two_pre = ST.n - NJ * NJ

        @block.vector
        def _(vector, g=g, xg=xg, two_pre=two_pre):
            q = Q(vector, SV)
            if DBG:
                q.wait(SG)
            for ot in range(NJ):
                q.wait(ST, two_pre + NJ * (ot + 1))
                bank = pb[5] if ot % 2 == 0 else pb[6]
                q.scalar_tensor_tensor(out=xg[:, ot * GT:(ot + 1) * GT],
                                       in0=bank[:], scalar=1.0,
                                       in1=xg[:, ot * GT:(ot + 1) * GT],
                                       op0=ALU.mult, op1=ALU.add)
        vxp_end = SV.n

        @block.gpsimd
        def _(gpsimd, g=g, gbase=gbase, vxp_end=vxp_end, xg=xg):
            q = Q(gpsimd, SG, dma=True)
            q.wait(SV, vxp_end)
            q.dma_start(
                out=xp_sp[:, gbase:gbase + GT].rearrange("(j p) t -> p j t",
                                                         p=128),
                in_=xg[:].rearrange("p (j t) -> p j t", j=NJ))
        spill_sg[g] = SG.n

        if g + 1 < NG:
            @block.sync
            def _(sync, g=g, vxp_end=vxp_end):
                q = Q(sync, SY, dma=True)
                q.wait(SV, vxp_end)
                if g - 1 in spill_sg:
                    q.wait(SG, spill_sg[g - 1])
                q.dma_start(
                    out=xg_s[(g + 1) % 2][:].rearrange("p (j t) -> p j t", j=NJ),
                    in_=xT[:, (g + 1) * GT:(g + 2) * GT].rearrange(
                        "(j p) t -> p j t", p=128))
                xload_sy[g + 1] = SY.n

        t_pre2 = ST.n
        v_pre2 = SV.n

        @block.vector
        def _(vector, g=g, t_pre2=t_pre2, xg=xg):
            q = Q(vector, SV)
            q.wait(ST)
            for j in range(NJ):
                if j >= 2:
                    q.wait(ST, t_pre2 + 2 * (j - 1))
                xj = xg[:, j * GT:(j + 1) * GT]
                q.tensor_copy(out=pairx[j % 2][:], in_=xj)
                q.tensor_tensor(out=pairq[j % 2][:], in0=pairx[j % 2][:],
                                in1=pairx[j % 2][:], op=ALU.mult)

        @block.tensor
        def _(tensor, g=g, v_pre2=v_pre2):
            q = Q(tensor, ST, serialize=False)
            for j in range(NJ):
                q.wait(SV, v_pre2 + 2 * (j + 1))
                q.matmul(out=pb[5][0:1, :], lhsT=oneskb_s[:], rhs=pairx[j % 2][:],
                         start=(j == 0), stop=(j == NJ - 1))
                q.matmul(out=pb[6][0:1, :], lhsT=oneskb_s[:], rhs=pairq[j % 2][:],
                         start=(j == 0), stop=(j == NJ - 1))

        @block.vector
        def _(vector, g=g):
            q = Q(vector, SV)
            q.wait(ST)
            q.tensor_scalar(out=mrow_s[:], in0=pb[5][0:1, :], scalar1=1.0 / D,
                            scalar2=None, op0=ALU.mult)
            q.tensor_scalar(out=vrow_s[:], in0=pb[6][0:1, :], scalar1=1.0 / D,
                            scalar2=EPS, op0=ALU.mult, op1=ALU.add)
            q.tensor_tensor(out=srow_s[:], in0=mrow_s[:], in1=mrow_s[:],
                            op=ALU.mult)
            q.tensor_tensor(out=vrow_s[:], in0=vrow_s[:], in1=srow_s[:],
                            op=ALU.subtract)

        @block.scalar
        def _(scalar, g=g, gbase=gbase):
            q = Q(scalar, SS)
            q.wait(SV)
            q.activation(out=vrow_s[:], in_=vrow_s[:], func=AF.Ln)
            q.activation(out=r2row_s[0:1, gbase:gbase + GT], in_=vrow_s[:],
                         func=AF.Exp, scale=-0.5)

        @block.vector
        def _(vector, g=g, gbase=gbase):
            q = Q(vector, SV)
            q.wait(SS)
            q.tensor_tensor(out=mr2row_s[0:1, gbase:gbase + GT], in0=mrow_s[:],
                            in1=r2row_s[0:1, gbase:gbase + GT], op=ALU.mult)

        @block.tensor
        def _(tensor, g=g, gbase=gbase):
            q = Q(tensor, ST, serialize=False)
            q.wait(SV)
            q.matmul(out=pb[5][:], lhsT=ones1_s[:],
                     rhs=r2row_s[0:1, gbase:gbase + GT], start=True, stop=True)
            q.matmul(out=pb[6][:], lhsT=ones1_s[:],
                     rhs=mr2row_s[0:1, gbase:gbase + GT], start=True, stop=True)

        @block.vector
        def _(vector, g=g, xg=xg):
            q = Q(vector, SV)
            q.wait(ST)
            for j in range(NJ):
                xj = xg[:, j * GT:(j + 1) * GT]
                q.tensor_tensor(out=tmp1_s[:], in0=xj, in1=pb[5][:],
                                op=ALU.mult)
                q.tensor_tensor(out=tmp1_s[:], in0=tmp1_s[:], in1=pb[6][:],
                                op=ALU.subtract)
                q.reduce_max(out=redt_s[:], in_=tmp1_s[:], axis=AX.X,
                             apply_absolute_value=True)
                q.tensor_tensor(out=x2max_s[:], in0=x2max_s[:], in1=redt_s[:],
                                op=ALU.max)

    attn_v = SV.n
    attn_t = ST.n
    attn_s = SS.n
    attn_g = SG.n
    attn_y = SY.n

    if stage == "attn":
        @block.gpsimd
        def _(gpsimd):
            q = Q(gpsimd, SG, dma=True)
            q.wait(SG, attn_g)
            for g in range(NG):
                q.dma_start(out=outT[:, g * GT:(g + 1) * GT],
                            in_=xp_sp[:, g * GT:(g + 1) * GT])
            gpsimd.wait_ge(SG.h, SG.n)
        blk.__exit__(None, None, None)
        return nc

    NGB = NOB // 4
    HALF = 65536.0

    def allreduce(idx, gm_col, pre_v, post):
        @block.tensor
        def _(tensor):
            q = Q(tensor, ST, serialize=False)
            q.wait(SV, max(pre_v, SV.n))
            q.matmul(out=pb[6][0:1, 0:128], lhsT=gm_col[:], rhs=idnf_s[:],
                     start=True, stop=True)

        @block.vector
        def _(vector):
            q = Q(vector, SV)
            q.wait(ST)
            q.reduce_max(out=sc_s[0:1, L0:L0 + 1], in_=pb[6][0:1, 0:128],
                         axis=AX.X)

        @block.gpsimd
        def _(gpsimd):
            q = Q(gpsimd, SG, dma=True)
            q.wait(SV)
            q.dma_start(out=cc_in[idx], in_=sc_s[0:1, L0:L0 + 1])
            gpsimd.wait_ge(SG.h, SG.n)
            ins = gpsimd.collective_compute(
                "AllReduce", ALU.max,
                replica_groups=[list(range(ncores))],
                ins=[cc_in[idx]], outs=[cc_out[idx]])
            ins.then_inc(SC.h, 1)
            SC.n += 1
            gpsimd.wait_ge(SC.h, SC.n)
            q.dma_start(out=sc_s[0:1, L1:L1 + 1], in_=cc_out[idx])

        @block.vector
        def _(vector):
            q = Q(vector, SV)
            q.wait(SG)
            post(q)

    def bcast_vec(src_lo, n, dst_lo):
        @block.tensor
        def _(tensor):
            q = Q(tensor, ST, serialize=False)
            q.wait(SV)
            q.matmul(out=pb[6][:, 0:n], lhsT=ones1_s[:],
                     rhs=sc_s[0:1, src_lo:src_lo + n], start=True, stop=True)

        @block.vector
        def _(vector):
            q = Q(vector, SV)
            q.wait(ST)
            q.tensor_copy(out=vec_s[:, dst_lo:dst_lo + n],
                          in_=pb[6][:, 0:n])

    L0, L1 = 0, 1

    def post_s1(q):
        q.tensor_scalar(out=sc_s[0:1, 2:3], in0=sc_s[0:1, 1:2],
                        scalar1=1.0 / 32767.0, scalar2=1e-12,
                        op0=ALU.mult, op1=ALU.max)
        q.reciprocal(out=sc_s[0:1, 3:4], in_=sc_s[0:1, 2:3])
        q.tensor_scalar(out=sc_s[0:1, 4:5], in0=sc_s[0:1, 3:4],
                        scalar1=-1.0, scalar2=None, op0=ALU.mult)
        q.tensor_scalar(out=r2row_s[:], in0=r2row_s[:],
                        scalar1=sc_s[0:1, 3:4], scalar2=None, op0=ALU.mult)
        q.tensor_scalar(out=mr2row_s[:], in0=mr2row_s[:],
                        scalar1=sc_s[0:1, 4:5], scalar2=None, op0=ALU.mult)

    allreduce(0, x2max_s, attn_v, post_s1)


    f2_bc, f2_p, f2_v, f2_sb = {}, {}, {}, {}
    for tt in range(NG):
        @block.sync
        def _(sync, tt=tt):
            q = Q(sync, SY, dma=True)
            if tt == 0:
                q.wait(ST, attn_t)
                q.wait(SG, attn_g)
            else:
                q.wait(SB, f2_sb[tt - 1])
            q.dma_start(out=xpr_s[:].rearrange("p (j t) -> p j t", j=NJ),
                        in_=xp_sp[:, tt * GT:(tt + 1) * GT].rearrange(
                            "(j p) t -> p j t", p=128))

        @block.tensor
        def _(tensor, tt=tt):
            q = Q(tensor, ST, serialize=False)
            q.wait(SV)
            q.matmul(out=pb[4][:], lhsT=ones1_s[:],
                     rhs=r2row_s[0:1, tt * GT:(tt + 1) * GT],
                     start=True, stop=True)
            q.matmul(out=pb[5][:], lhsT=ones1_s[:],
                     rhs=mr2row_s[0:1, tt * GT:(tt + 1) * GT],
                     start=True, stop=True)

        @block.vector
        def _(vector, tt=tt):
            q = Q(vector, SV)
            q.wait(SY)
            q.wait(ST)
            if tt > 0:
                q.wait(SB, f2_sb[tt - 1])
            q.tensor_copy(out=r1bc_s[:], in_=pb[4][:])
            q.tensor_copy(out=mr1bc_s[:], in_=pb[5][:])
        f2_bc[tt] = SV.n

        for j in range(NJ):
            jj = 8 * tt + j

            @block.gpsimd
            def _(gpsimd, tt=tt, j=j, jj=jj):
                q = Q(gpsimd, SB)
                if j == 0:
                    q.wait(SV, f2_bc[tt])
                if jj >= 2:
                    q.wait(SV, f2_v[jj - 2])
                q.tensor_tensor(out=xf2[j % 2][:],
                                in0=xpr_s[:, j * GT:(j + 1) * GT],
                                in1=r1bc_s[:], op=ALU.mult)
                f2_p[jj] = SB.n

            @block.vector
            def _(vector, tt=tt, j=j, jj=jj):
                q = Q(vector, SV)
                q.wait(SB, f2_p[jj])
                q.scalar_tensor_tensor(out=xfi[:], in0=xf2[j % 2][:],
                                       scalar=1.0, in1=mr1bc_s[:],
                                       op0=ALU.mult, op1=ALU.add)
                q.tensor_copy(out=x1b_s[:, j * TOK + tt * GT:
                                        j * TOK + (tt + 1) * GT],
                              in_=xfi[:])
                f2_v[jj] = SV.n
        f2_sb[tt] = SB.n
    x1_done_v = SV.n

    NGT = NG // 4

    def big_mm_phase(ubase, dst_sp, gm_col, first, ii=None):
        t0 = ST.n
        y_wt = {}
        y_g = {}
        evac_ss = {}
        spill_sg = {}
        iispill_sg = {}
        vdone = {}
        v1_done = {}
        v3_done = {}
        s_done = {}

        def ii_rd(tg):
            grp, tile = divmod(tg, 2)
            return rd_s[grp % 4][:, tile * 1024:(tile + 1) * 1024]

        def ii_t45a(tg):
            grp, tile = divmod(tg, 2)
            return t45a_s[grp % 2][:, tile * 1024:(tile + 1) * 1024]

        @block.sync
        def _(sync):
            q = Q(sync, SY, dma=True)
            if first:
                q.wait(SC, deq_done_c)
            q.wait(ST)
            q.dma_start(
                out=wt_s[0][:].rearrange("p (j o) -> p j o", j=NJ),
                in_=wgat[ubase * 1024:(ubase + 1) * 1024, :].rearrange(
                    "(j p) o -> p j o", p=128))
            y_wt[0] = SY.n
            if ii:
                q.wait(SG)
                for grp in range(4):
                    ob0, ttg0 = divmod(grp, NGT)
                    q.dma_start(out=rd_s[grp % 4][:],
                                in_=ii["src"][ob0 * 128:(ob0 + 1) * 128,
                                              ttg0 * 4 * GT:(ttg0 + 1) * 4 * GT])
                    y_g[grp] = SY.n

        for ob in range(NOB):
            @block.sync
            def _(sync, ob=ob):
                q = Q(sync, SY, dma=True)
                if ob + 1 < NOB:
                    q.wait(ST, t0 + 64 * ob)
                    q.dma_start(
                        out=wt_s[(ob + 1) % 2][:].rearrange("p (j o) -> p j o",
                                                            j=NJ),
                        in_=wgat[(ubase + ob + 1) * 1024:
                                 (ubase + ob + 2) * 1024, :].rearrange(
                            "(j p) o -> p j o", p=128))
                    y_wt[ob + 1] = SY.n

            @block.tensor
            def _(tensor, ob=ob):
                q = Q(tensor, ST, serialize=False)
                if ob == 0:
                    q.wait(SV)
                q.wait(SY, y_wt[ob])
                for half in range(2):
                    if ob > 0:
                        q.wait(SS, evac_ss[(ob - 1, half)])
                    for j in range(NJ):
                        for t4 in range(4):
                            tt = half * 4 + t4
                            q.matmul(
                                out=pb[tt][:],
                                lhsT=wt_s[ob % 2][:, j * 128:(j + 1) * 128],
                                rhs=x1b_s[:, j * TOK + tt * GT:
                                          j * TOK + (tt + 1) * GT],
                                start=(j == 0), stop=(j == NJ - 1))

            def emit_evac(ob, half):
                @block.scalar
                def _(scalar, ob=ob, half=half):
                    q = Q(scalar, SS)
                    for t4 in range(4):
                        tt = half * 4 + t4
                        q.wait(ST, t0 + 64 * ob + 32 * half + 28 + t4 + 1)
                        if ob > 0:
                            q.wait(SG, spill_sg[(ob - 1, half)])
                            q.wait(SV, vdone[(ob - 1, half)])
                        q.copy(out=stg_s[half][:, t4 * GT:(t4 + 1) * GT],
                               in_=pb[tt][:])
                        if t4 == 3:
                            evac_ss[(ob, half)] = SS.n

            def ii_v23(q, tg):
                grp, tile = divmod(tg, 2)
                p = tg % 2
                if tile == 0 and grp >= 2:
                    q.wait(SG, iispill_sg[grp - 2])
                q.wait(SS, s_done[tg])
                q.tensor_copy(out=sib[p][:], in_=sfb[p][:])
                q.scalar_tensor_tensor(out=ii_t45a(tg), in0=gii[p][:],
                                       scalar=0.0, in1=sib[p][:],
                                       op0=ALU.add, op1=ALU.mult)
                v3_done[tg] = SV.n
                if tile == 1:
                    q.reduce_max(out=redt_s[:], in_=t45a_s[grp % 2][:],
                                 axis=AX.X, apply_absolute_value=True)
                    q.tensor_tensor(out=ii["gm"][:], in0=ii["gm"][:],
                                    in1=redt_s[:], op=ALU.max)

            def emit_ii_spill(gsp):
                @block.gpsimd
                def _(gpsimd, gsp=gsp):
                    q = Q(gpsimd, SG, dma=True)
                    q.wait(SV, v3_done[2 * gsp + 1])
                    ob0, ttg0 = divmod(gsp, NGT)
                    q.dma_start(
                        out=ii["dst"][ob0 * 128:(ob0 + 1) * 128,
                                      ttg0 * 4 * GT:(ttg0 + 1) * 4 * GT],
                        in_=t45a_s[gsp % 2][:])
                    iispill_sg[gsp] = SG.n

            if ii:
                for sub in range(2):
                    grp = 2 * ob + sub
                    for tile in range(2):
                        tg = 2 * grp + tile

                        @block.vector
                        def _(vector, tg=tg, grp=grp, tile=tile):
                            q = Q(vector, SV)
                            if tile == 0:
                                q.wait(SY, y_g[grp])
                            if tg >= 2:
                                q.wait(SS, s_done[tg - 2])
                            q.tensor_scalar(out=gii[tg % 2][:], in0=ii_rd(tg),
                                            scalar1=vec_s[:, 1:2],
                                            scalar2=None, op0=ALU.mult)
                            v1_done[tg] = SV.n
                            if tg >= 1:
                                ii_v23(q, tg - 1)

                        @block.scalar
                        def _(scalar, tg=tg):
                            q = Q(scalar, SS)
                            q.wait(SV, v1_done[tg])
                            if tg >= 2:
                                q.wait(SV, v3_done[tg - 2])
                            p = tg % 2
                            q.activation(out=sfb[p][:], in_=gii[p][:],
                                         func=AF.Exp, scale=vec_s[:, 2:3],
                                         bias=vec_s[:, 3:4])
                            q.activation(out=sfb[p][:], in_=sfb[p][:],
                                         func=AF.Ln, bias=vec_s[:, 9:10])
                            q.activation(out=sfb[p][:], in_=sfb[p][:],
                                         func=AF.Exp, scale=-1.0)
                            s_done[tg] = SS.n

                        if tg >= 1 and (tg - 1) % 2 == 1:
                            emit_ii_spill((tg - 1) // 2)

                    emit_evac(ob, sub)
            else:
                emit_evac(ob, 0)
                emit_evac(ob, 1)

            if ii:
                @block.sync
                def _(sync, ob=ob):
                    q = Q(sync, SY, dma=True)
                    for grp in (2 * ob + 4, 2 * ob + 5):
                        if grp < 2 * NOB:
                            q.wait(SV, v1_done[2 * (grp - 4) + 1])
                            ob0, ttg0 = divmod(grp, NGT)
                            q.dma_start(
                                out=rd_s[grp % 4][:],
                                in_=ii["src"][ob0 * 128:(ob0 + 1) * 128,
                                              ttg0 * 4 * GT:(ttg0 + 1) * 4 * GT])
                            y_g[grp] = SY.n

            @block.vector
            def _(vector, ob=ob):
                q = Q(vector, SV)
                for half in range(2):
                    q.wait(SS, evac_ss[(ob, half)])
                    q.reduce_max(out=redt_s[:], in_=stg_s[half][:], axis=AX.X,
                                 apply_absolute_value=True)
                    q.tensor_tensor(out=gm_col[:], in0=gm_col[:],
                                    in1=redt_s[:], op=ALU.max)
                    vdone[(ob, half)] = SV.n

            @block.gpsimd
            def _(gpsimd, ob=ob):
                q = Q(gpsimd, SG, dma=True)
                for half in range(2):
                    q.wait(SS, evac_ss[(ob, half)])
                    q.dma_start(
                        out=dst_sp[ob * 128:(ob + 1) * 128,
                                   half * 4 * GT:(half + 1) * 4 * GT],
                        in_=stg_s[half][:])
                    spill_sg[(ob, half)] = SG.n

        if ii:
            @block.vector
            def _(vector):
                q = Q(vector, SV)
                ii_v23(q, 4 * NOB - 1)
            emit_ii_spill(2 * NOB - 1)
        return SV.n

    gate_done_v = big_mm_phase(0, g_sp, gmG_s, True)

    def post_s3(q):
        q.tensor_scalar(out=sc_s[0:1, 7:8], in0=sc_s[0:1, 6:7],
                        scalar1=sc_s[0:1, 2:3], scalar2=None, op0=ALU.mult)
        q.tensor_scalar(out=sc_s[0:1, 8:9], in0=sc_s[0:1, 7:8],
                        scalar1=1.0 / 32767.0, scalar2=1e-12,
                        op0=ALU.mult, op1=ALU.max)
        q.reciprocal(out=sc_s[0:1, 9:10], in_=sc_s[0:1, 8:9])
        q.tensor_scalar(out=sc_s[0:1, 10:11], in0=sc_s[0:1, 9:10],
                        scalar1=sc_s[0:1, 2:3], scalar2=None, op0=ALU.mult)
        q.tensor_scalar(out=sc_s[0:1, 11:12], in0=sc_s[0:1, 8:9],
                        scalar1=-1.0, scalar2=None, op0=ALU.mult)
        q.tensor_scalar(out=sc_s[0:1, 22:23], in0=sc_s[0:1, 8:9],
                        scalar1=1.0 / HALF, scalar2=None, op0=ALU.mult)

    L0, L1 = 5, 6
    allreduce(1, gmG_s, gate_done_v, post_s3)
    bcast_vec(10, 2, 1)

    up_done_v = big_mm_phase(NOB, u_sp, gmU_s, False,
                             ii={"src": g_sp, "dst": t4_sp, "gm": gmT4_s})
    ii_done_v = up_done_v

    def post_s2(q):
        q.tensor_scalar(out=sc_s[0:1, 15:16], in0=sc_s[0:1, 14:15],
                        scalar1=sc_s[0:1, 2:3], scalar2=None, op0=ALU.mult)
        q.tensor_scalar(out=sc_s[0:1, 16:17], in0=sc_s[0:1, 15:16],
                        scalar1=1.0 / 32767.0, scalar2=1e-12,
                        op0=ALU.mult, op1=ALU.max)
        q.reciprocal(out=sc_s[0:1, 17:18], in_=sc_s[0:1, 16:17])
        q.tensor_scalar(out=sc_s[0:1, 18:19], in0=sc_s[0:1, 17:18],
                        scalar1=sc_s[0:1, 2:3], scalar2=None, op0=ALU.mult)

    L0, L1 = 13, 14
    allreduce(2, gmU_s, up_done_v, post_s2)
    bcast_vec(18, 1, 4)

    def stage_iii(gm_col):
        y0 = SY.n
        v0 = SV.n
        d4 = {}
        s_ui = {}
        v_stt = {}
        g_spill = {}
        ubufs = (rd_s[0], rd_s[2], stg_s[0], stg_s[1])
        tbufs = (rd_s[1], rd_s[3], t45a_s[0], t45a_s[1])

        for grp in range(NOB * NGT):
            ob, ttg = divmod(grp, NGT)

            @block.sync
            def _(sync, grp=grp, ob=ob, ttg=ttg):
                q = Q(sync, SY, dma=True)
                if grp == 0:
                    q.wait(SG)
                    q.wait(SV, v0)
                if grp >= 4:
                    q.wait(SS, s_ui[2 * (grp - 4) + 1])
                q.dma_start(out=ubufs[grp % 4][:],
                            in_=u_sp[ob * 128:(ob + 1) * 128,
                                     ttg * 4 * GT:(ttg + 1) * 4 * GT])

            if grp < 2:
                @block.gpsimd
                def _(gpsimd, grp=grp, ob=ob, ttg=ttg):
                    q = Q(gpsimd, SD3, dma=True)
                    if grp == 0:
                        q.wait(SG)
                        q.wait(SV, v0)
                    q.dma_start(out=tbufs[grp % 4][:],
                                in_=t4_sp[ob * 128:(ob + 1) * 128,
                                          ttg * 4 * GT:(ttg + 1) * 4 * GT])
                d4[grp] = SD3.n

            @block.scalar
            def _(scalar, grp=grp):
                q = Q(scalar, SS)
                if grp == 0:
                    q.wait(SV, v0)
                q.wait(SY, y0 + 16 * (grp + 1))
                for tile in range(2):
                    tg = 2 * grp + tile
                    if tg >= 2:
                        q.wait(SV, v_stt[tg - 2])
                    q.activation(
                        out=gii[tg % 2][:], func=AF.Copy,
                        in_=ubufs[grp % 4][:, tile * 1024:(tile + 1) * 1024],
                        scale=vec_s[:, 4:5])
                    s_ui[tg] = SS.n

            @block.vector
            def _(vector, grp=grp):
                q = Q(vector, SV)
                q.wait(SD3, d4[grp])
                for tile in range(2):
                    tg = 2 * grp + tile
                    q.tensor_scalar(
                        out=sib[tg % 2][:],
                        in0=tbufs[grp % 4][:, tile * 1024:(tile + 1) * 1024],
                        scalar1=vec_s[:, 5:6], scalar2=None, op0=ALU.mult)
                for tile in range(2):
                    tg = 2 * grp + tile
                    q.wait(SS, s_ui[tg])
                    if tile == 0 and grp >= 2:
                        q.wait(SG, g_spill[grp - 2])
                    q.scalar_tensor_tensor(
                        out=t45g_s[grp % 2][:, tile * 1024:(tile + 1) * 1024],
                        in0=gii[tg % 2][:], scalar=0.0, in1=sib[tg % 2][:],
                        op0=ALU.add, op1=ALU.mult)
                    v_stt[tg] = SV.n
                q.reduce_max(out=redt_s[:], in_=t45g_s[grp % 2][:], axis=AX.X,
                             apply_absolute_value=True)
                q.tensor_tensor(out=gm_col[:], in0=gm_col[:], in1=redt_s[:],
                                op=ALU.max)

            @block.gpsimd
            def _(gpsimd, grp=grp, ob=ob, ttg=ttg):
                if grp + 2 < NOB * NGT:
                    g2 = grp + 2
                    ob2, ttg2 = divmod(g2, NGT)
                    qd = Q(gpsimd, SD3, dma=True)
                    if g2 >= 4:
                        qd.wait(SV, v_stt[2 * (g2 - 4) + 1])
                    qd.dma_start(out=tbufs[g2 % 4][:],
                                 in_=t4_sp[ob2 * 128:(ob2 + 1) * 128,
                                           ttg2 * 4 * GT:(ttg2 + 1) * 4 * GT])
                    d4[g2] = SD3.n
                q = Q(gpsimd, SG, dma=True)
                q.wait(SV, v_stt[2 * grp + 1])
                q.dma_start(
                    out=t5_sp[ob * 128:(ob + 1) * 128,
                              ttg * 4 * GT:(ttg + 1) * 4 * GT],
                    in_=t45g_s[grp % 2][:])
                g_spill[grp] = SG.n
        return SV.n

    def post_s4(q):
        q.tensor_scalar(out=sc_s[0:1, 21:22], in0=sc_s[0:1, 20:21],
                        scalar1=sc_s[0:1, 22:23], scalar2=None, op0=ALU.mult)
        q.tensor_scalar(out=sc_s[0:1, 23:24], in0=sc_s[0:1, 21:22],
                        scalar1=1.0 / 32767.0, scalar2=1e-12,
                        op0=ALU.mult, op1=ALU.max)
        q.reciprocal(out=sc_s[0:1, 24:25], in_=sc_s[0:1, 23:24])
        q.tensor_scalar(out=sc_s[0:1, 25:26], in0=sc_s[0:1, 24:25],
                        scalar1=sc_s[0:1, 22:23], scalar2=None, op0=ALU.mult)

    L0, L1 = 19, 20
    allreduce(3, gmT4_s, ii_done_v, post_s4)
    bcast_vec(25, 1, 5)

    iii_done_v = stage_iii(gmT5_s)

    def post_s5(q):
        q.tensor_scalar(out=sc_s[0:1, 28:29], in0=sc_s[0:1, 23:24],
                        scalar1=sc_s[0:1, 16:17], scalar2=None, op0=ALU.mult)
        q.tensor_scalar(out=sc_s[0:1, 29:30], in0=sc_s[0:1, 27:28],
                        scalar1=sc_s[0:1, 28:29], scalar2=None, op0=ALU.mult)
        q.tensor_scalar(out=sc_s[0:1, 30:31], in0=sc_s[0:1, 29:30],
                        scalar1=1.0 / 32767.0, scalar2=1e-12,
                        op0=ALU.mult, op1=ALU.max)
        q.reciprocal(out=sc_s[0:1, 31:32], in_=sc_s[0:1, 30:31])
        q.tensor_scalar(out=sc_s[0:1, 26:27], in0=sc_s[0:1, 31:32],
                        scalar1=sc_s[0:1, 28:29], scalar2=None, op0=ALU.mult)
        q.tensor_copy(out=sc_s[0:1, 27:28], in_=sc_s[0:1, 30:31])

    L0, L1 = 26, 27
    allreduce(4, gmT5_s, iii_done_v, post_s5)
    bcast_vec(26, 2, 6)

    @block.sync
    def _(sync):
        q = Q(sync, SY, dma=True)
        q.wait(SC, deq_done_c)
        q.wait(ST)
        q.wait(SV)
        wdv = wdT_s[:].rearrange("p (iob obb o) -> p iob obb o",
                                 iob=NOB, obb=NJ, o=128)
        for obb in range(NJ):
            q.dma_start(
                out=wdv[:, :, obb, :],
                in_=wdgat[obb * 3 * 1024:(obb + 1) * 3 * 1024, :].rearrange(
                    "(c p) o -> p c o", p=128))
    wd_done_y = SY.n

    iv_s0 = SS.n
    iv_v0 = SV.n
    iv_t0 = ST.n
    iv_g0 = SG.n
    NV4 = NOB + NJ
    NT4 = NJ * NOB
    y4 = {}
    s4c = {}
    v4c = {}

    for tt in range(NG):
        @block.sync
        def _(sync, tt=tt):
            q = Q(sync, SY, dma=True)
            if tt == 0:
                q.wait(SV)
                q.wait(SG)
            else:
                q.wait(SV, iv_v0 + NV4 * tt)
            q.dma_start(out=xpr_s[:].rearrange("p (j t) -> p j t", j=NJ),
                        in_=xp_sp[:, tt * GT:(tt + 1) * GT].rearrange(
                            "(j p) t -> p j t", p=128))
            for ob in range(0, 8, 2):
                if tt > 0:
                    q.wait(SS, s4c[(tt - 1, ob + 16)])
                q.dma_start(
                    out=rd_s[ob % 2][:, ((ob // 2) % 4) * GT:
                                     ((ob // 2) % 4 + 1) * GT],
                    in_=t5_sp[ob * 128:(ob + 1) * 128,
                              tt * GT:(tt + 1) * GT])
                y4[(tt, ob)] = (SY, SY.n)

        @block.gpsimd
        def _(gpsimd, tt=tt):
            q = Q(gpsimd, SD2, dma=True)
            if tt == 0:
                q.wait(SV, iv_v0)
                q.wait(SG, iv_g0)
            for ob in range(1, 8, 2):
                if tt > 0:
                    q.wait(SS, s4c[(tt - 1, ob + 16)])
                q.dma_start(
                    out=rd_s[ob % 2][:, ((ob // 2) % 4) * GT:
                                     ((ob // 2) % 4 + 1) * GT],
                    in_=t5_sp[ob * 128:(ob + 1) * 128,
                              tt * GT:(tt + 1) * GT])
                y4[(tt, ob)] = (SD2, SD2.n)

        def emit_load(tt, ob):
            blk_dec = block.sync if ob % 2 == 0 else block.gpsimd
            @blk_dec
            def _(eng, tt=tt, ob=ob):
                q = Q(eng, SY if ob % 2 == 0 else SD2, dma=True)
                q.wait(SS, s4c[(tt, ob - 8)])
                q.dma_start(
                    out=rd_s[ob % 2][:, ((ob // 2) % 4) * GT:
                                     ((ob // 2) % 4 + 1) * GT],
                    in_=t5_sp[ob * 128:(ob + 1) * 128,
                              tt * GT:(tt + 1) * GT])
                y4[(tt, ob)] = (SY, SY.n) if ob % 2 == 0 else (SD2, SD2.n)

        for ob in range(NOB):
            @block.scalar
            def _(scalar, tt=tt, ob=ob):
                q = Q(scalar, SS)
                wI = (iiAi, iiBi)[ob % 2]
                sem, n = y4[(tt, ob)]
                q.wait(sem, n)
                if tt == 0 and ob < 2:
                    q.wait(ST, iv_t0)
                if ob >= 2:
                    q.wait(SV, v4c[(tt, ob - 2)])
                elif tt > 0:
                    q.wait(SV, v4c[(tt - 1, 22 + ob)])
                slot = rd_s[ob % 2][:, ((ob // 2) % 4) * GT:
                                    ((ob // 2) % 4 + 1) * GT]
                q.activation(out=wI[0][:], in_=slot, func=AF.Copy,
                             scale=vec_s[:, 6:7])
                s4c[(tt, ob)] = SS.n

            if ob + 8 < NOB:
                emit_load(tt, ob + 8)

            @block.vector
            def _(vector, tt=tt, ob=ob):
                q = Q(vector, SV)
                wI = (iiAi, iiBi)[ob % 2]
                q.wait(SS, s4c[(tt, ob)])
                half, obl = divmod(ob, 12)
                if ob == 0 and tt > 0:
                    q.wait(ST, iv_t0 + NT4 * tt)
                if ob == 12:
                    q.wait(ST, iv_t0 + NT4 * tt + NT4 // 2)
                q.tensor_copy(out=o16h_s[:, obl * GT:(obl + 1) * GT],
                              in_=wI[0][:])
                v4c[(tt, ob)] = SV.n

        @block.tensor
        def _(tensor, tt=tt):
            q = Q(tensor, ST, serialize=False)
            if tt == 0:
                q.wait(SY, wd_done_y)
            for half in range(2):
                q.wait(SV, v4c[(tt, half * 12 + 11)])
                for ot in range(NJ):
                    if half == 0 and tt > 0:
                        q.wait(SV, iv_v0 + NV4 * (tt - 1) + NOB + ot + 1)
                    for i12 in range(12):
                        iob = half * 12 + i12
                        q.matmul(out=pb[ot][:],
                                 lhsT=wdT_s[:, iob * 1024 + ot * 128:
                                            iob * 1024 + (ot + 1) * 128],
                                 rhs=o16h_s[:, i12 * GT:(i12 + 1) * GT],
                                 start=(iob == 0), stop=(iob == NOB - 1))

        @block.vector
        def _(vector, tt=tt):
            q = Q(vector, SV)
            for ot in range(NJ):
                q.wait(ST, iv_t0 + NT4 * tt + NT4 // 2 + 12 * (ot + 1))
                slot = ost_s[(ot // 4) % 2][:, (ot % 4) * GT:(ot % 4 + 1) * GT]
                if tt > 0 or ot >= 4:
                    ndma = tt * 2 + ot // 4
                    q.wait(SG, iv_g0 + 16 * ndma)
                q.scalar_tensor_tensor(out=slot, in0=pb[ot][:],
                                       scalar=vec_s[:, 7:8],
                                       in1=xpr_s[:, ot * GT:(ot + 1) * GT],
                                       op0=ALU.mult, op1=ALU.add)

        @block.gpsimd
        def _(gpsimd, tt=tt):
            q = Q(gpsimd, SG, dma=True)
            for half in range(2):
                q.wait(SV, iv_v0 + NV4 * tt + NOB + 4 * (half + 1))
                q.dma_start(
                    out=outT[:, tt * GT:(tt + 1) * GT].rearrange(
                        "(j p) t -> p j t", p=128)[:, half * 4:(half + 1) * 4, :],
                    in_=ost_s[half][:].rearrange("p (o t) -> p o t", o=4))

    @block.gpsimd
    def _(gpsimd):
        gpsimd.wait_ge(SG.h, SG.n)

    blk.__exit__(None, None, None)
    return nc



def kernel(L0_attn_hidden_states, L0_attn_Wq, L0_attn_Wf, L0_attn_Wi,
           L0_attn_g_weight, L0_attn_Wo, L0_ffn_w_gate, L0_ffn_w_up,
           L0_ffn_w_down):
    import ml_dtypes
    x = np.asarray(L0_attn_hidden_states, dtype=np.float32)
    base = {
        "wqT": np.ascontiguousarray(np.asarray(L0_attn_Wq, np.float32).T),
        "wfT": np.ascontiguousarray(np.asarray(L0_attn_Wf, np.float32).T),
        "wiT": np.ascontiguousarray(np.asarray(L0_attn_Wi, np.float32).T),
        "woT": np.ascontiguousarray(np.asarray(L0_attn_Wo, np.float32).T),
        "c_gw": np.ascontiguousarray(
            np.asarray(L0_attn_g_weight, np.float32).reshape(NJ, 128).T),
        "c_idn": np.eye(128, dtype=ml_dtypes.bfloat16),
        "c_idnf": np.eye(128, dtype=np.float32),
        "c_mask": np.triu(np.ones((128, 128), ml_dtypes.bfloat16)),
        "c_ones1": np.ones((1, 128), np.float32),
        "c_onesk": np.ones((128, GT), ml_dtypes.bfloat16),
        "c_oneskb": np.ones((128, 1), ml_dtypes.bfloat16),
    }
    nc = build()
    wg = np.asarray(L0_ffn_w_gate, np.float32)
    wu = np.asarray(L0_ffn_w_up, np.float32)
    wd = np.asarray(L0_ffn_w_down, np.float32)
    upc = 2 * NOB // NCORES
    wupc = NJ * 3 // NCORES
    in_maps = []
    for c_i in range(NCORES):
        xc = x[B_CORE * c_i: B_CORE * (c_i + 1)]
        m = dict(base)
        m["xT"] = np.ascontiguousarray(xc.transpose(2, 0, 1).reshape(D, TOK))
        units = []
        for k in range(upc):
            ug = c_i * upc + k
            src = wg if ug < NOB else wu
            ob = ug % NOB
            units.append(src[ob * 128:(ob + 1) * 128, :])
        m["wdq"] = np.ascontiguousarray(np.concatenate(units, axis=0))
        units2 = []
        for k in range(wupc):
            ug = c_i * wupc + k
            obb, c3 = divmod(ug, 3)
            units2.append(wd[obb * 128:(obb + 1) * 128,
                             c3 * 1024:(c3 + 1) * 1024])
        m["wdq2"] = np.ascontiguousarray(np.concatenate(units2, axis=0))
        in_maps.append(m)
    res = run_bass_kernel_spmd(nc, in_maps, core_ids=list(range(NCORES)))
    out = np.empty((B_FULL, T, D), np.float32)
    for c_i in range(NCORES):
        o = res.results[c_i]["outT"].reshape(D, B_CORE, T).transpose(1, 2, 0)
        out[B_CORE * c_i: B_CORE * (c_i + 1)] = o
    return out
